# version 02
# baseline (speedup 1.0000x reference)
import numpy as np
import ml_dtypes
from contextlib import ExitStack

import concourse.bass as bass
import concourse.mybir as mybir
from concourse.bass_utils import run_bass_kernel_spmd

F32 = mybir.dt.float32
F16 = mybir.dt.float16
BF16 = mybir.dt.bfloat16
I32 = mybir.dt.int32
AF = mybir.ActivationFunctionType
ALU = mybir.AluOpType

D = 4096
KC = 32
PROJ = 18560
MPROJ = 145
DI = 8192
H = 128
P = 64
G = 8
N = 128
EPS = 1e-5

T = 2048
TH = 256
NCR = 16
NCH = 2
NC_ = NCR + NCH
TA = TH + T
PADC = 4
HW = PADC + TH
W_ALL = PADC + TA

INTER = 11008
INTP = 11264
NB = 88

ORD = [144] + list(range(128, 144)) + list(range(64, 128)) + list(range(64))
N_HBC = 81

DEBUG_N = 0


def sigma_d_perm():
    return np.arange(4096).reshape(512, 8).T.reshape(-1).copy()


def build_program(debug=False, nphase=9, p2n=None, no_bt=False, no_spill=False, p3n=64):
    nc = bass.Bass("TRN2", target_bir_lowering=False)
    ap = lambda t: t.ap()

    xin = ap(nc.dram_tensor("xin", [KC, 128, W_ALL], F32, kind="ExternalInput"))
    w1 = ap(nc.dram_tensor("w1", [MPROJ + 1, KC * 128 * 128], F32, kind="ExternalInput"))
    w2 = ap(nc.dram_tensor("w2", [KC, 64 * 128 * 128], F32, kind="ExternalInput"))
    normw = ap(nc.dram_tensor("normw", [128, KC], F32, kind="ExternalInput"))
    convw = ap(nc.dram_tensor("convw", [80, 128, 4], F32, kind="ExternalInput"))
    convb = ap(nc.dram_tensor("convb", [128, 80], F32, kind="ExternalInput"))
    dtbias = ap(nc.dram_tensor("dtbias", [128, 1], F32, kind="ExternalInput"))
    alog = ap(nc.dram_tensor("alog", [128, 1], F32, kind="ExternalInput"))
    dprep = ap(nc.dram_tensor("dprep", [128, 64], F32, kind="ExternalInput"))
    gnwi = ap(nc.dram_tensor("gnwi", [128, 64], F32, kind="ExternalInput"))
    flag = ap(nc.dram_tensor("flag", [128, 1], F32, kind="ExternalInput"))
    gq = ap(nc.dram_tensor("gq", [512, INTP], I32, kind="ExternalInput"))
    gsr = ap(nc.dram_tensor("gsr", [512, INTP], F16, kind="ExternalInput"))
    gz = ap(nc.dram_tensor("gz", [32, 1376], I32, kind="ExternalInput"))
    gso = ap(nc.dram_tensor("gso", [32, INTP], F16, kind="ExternalInput"))
    uq = ap(nc.dram_tensor("uq", [512, INTP], I32, kind="ExternalInput"))
    usr = ap(nc.dram_tensor("usr", [512, INTP], F16, kind="ExternalInput"))
    uz = ap(nc.dram_tensor("uz", [32, 1376], I32, kind="ExternalInput"))
    uso = ap(nc.dram_tensor("uso", [32, INTP], F16, kind="ExternalInput"))
    dq = ap(nc.dram_tensor("dq", [1408, D], I32, kind="ExternalInput"))
    dsr = ap(nc.dram_tensor("dsr", [1408, D], F16, kind="ExternalInput"))
    dz = ap(nc.dram_tensor("dz", [86, 512], I32, kind="ExternalInput"))
    dso = ap(nc.dram_tensor("dso", [86, D], F16, kind="ExternalInput"))

    out = ap(nc.dram_tensor("out", [KC, 128, T], F32, kind="ExternalOutput"))
    dbg = ap(nc.dram_tensor("dbg", [16, 128, 2304], F32, kind="ExternalOutput")) \
        if debug else None

    d_hidt = ap(nc.dram_tensor("d_hidt", [64 * NC_, 128, 128], BF16, kind="Internal"))
    d_hidc = ap(nc.dram_tensor("d_hidc", [64 * NC_, 128, 128], BF16, kind="Internal"))
    d_b = ap(nc.dram_tensor("d_b", [G, NC_, 128, 128], BF16, kind="Internal"))
    d_bt = ap(nc.dram_tensor("d_bt", [G, NC_, 128, 128], BF16, kind="Internal"))
    d_c = ap(nc.dram_tensor("d_c", [G, NC_, 128, 128], BF16, kind="Internal"))
    d_chl = ap(nc.dram_tensor("d_chl", [2, 128, TA], BF16, kind="Internal"))
    d_ncat = ap(nc.dram_tensor("d_ncat", [128, NC_ * 128], F32, kind="Internal"))
    d_sg = ap(nc.dram_tensor("d_sg", [64, NCR, 128, 128], F32, kind="Internal"))
    d_yn = ap(nc.dram_tensor("d_yn", [64, 128, T], BF16, kind="Internal"))
    d_x1 = ap(nc.dram_tensor("d_x1", [KC, 128, T], F32, kind="Internal"))
    d_y1 = ap(nc.dram_tensor("d_y1", [KC, 128, T], F32, kind="Internal"))
    d_int = ap(nc.dram_tensor("d_int", [NB, 128, T], BF16, kind="Internal"))
    d_wd = ap(nc.dram_tensor("d_wd", [NB, 128, D], BF16, kind="Internal"))
    d_x2 = ap(nc.dram_tensor("d_x2", [KC, 128, T], F32, kind="Internal"))
    d_bnc = ap(nc.dram_tensor("d_bnc", [8, 4096], F32, kind="Internal"))

    with ExitStack() as ctx:
        def sbt(name, shape, dt):
            return ctx.enter_context(nc.sbuf_tensor(name, shape, dt))

        def pst(name, shape, dt=F32):
            return ctx.enter_context(nc.psum_tensor(name, shape, dt))

        _scoped = []

        def sbt_s(name, shape, dt):
            cm = nc.sbuf_tensor(name, shape, dt)
            _scoped.append(cm)
            return cm.__enter__()

        def pst_s(name, shape, dt=F32):
            cm = nc.psum_tensor(name, shape, dt)
            _scoped.append(cm)
            return cm.__enter__()

        def free_scope():
            for cm in reversed(_scoped):
                cm.__exit__(None, None, None)
            _scoped.clear()

        def mark_scope():
            return len(_scoped)

        def free_to(mk):
            while len(_scoped) > mk:
                _scoped.pop().__exit__(None, None, None)

        SEMS = {}
        V = {}

        def sem(name):
            SEMS[name] = ctx.enter_context(nc.semaphore(name))
            V[name] = 0

        for nm in ["s_gp", "s_sy", "s_sc", "s_sy2", "s_gpa", "s_gpb",
                   "s_sca", "s_scb", "s_h0", "s_h1", "s_c0", "s_c1",
                   "s_m0", "s_m1", "s_g0", "s_g1", "s_cc", "s_yn",
                   "e_pe", "e_dve", "e_sc", "e_pl",
                   "e_pe2", "e_dve2", "e_sc2", "e_pl2"]:
            sem(nm)

        def inc(ins, name, k=1):
            ins.then_inc(SEMS[name], k)
            V[name] += k
            return V[name]

        def wait(eng, name, v=None):
            eng.wait_ge(SEMS[name], V[name] if v is None else v)

        g_raw = nc.gpsimd
        v_raw = nc.vector
        s_raw = nc.scalar
        t = nc.tensor
        sy = nc.sync

        _SELF = {}

        class _SS:
            _PASS = {"wait_ge", "dma_start", "sem_inc", "If", "Else", "snap"}

            def __init__(self, eng, sems):
                self._e = eng
                self._s = sems

            def __getattr__(self, name):
                real = getattr(self._e, name)
                if name in self._PASS or not callable(real):
                    return real

                def wrapped(*a, **k):
                    for sn in self._s:
                        if V[sn] > 0:
                            self._e.wait_ge(SEMS[sn], V[sn])
                    return real(*a, **k)
                return wrapped

        g = _SS(g_raw, ["e_pl", "e_pl2"])
        v = _SS(v_raw, ["e_dve", "e_dve2"])
        s = _SS(s_raw, ["e_sc", "e_sc2"])
        ESEM = {id(v): "e_dve", id(g): "e_pl", id(s): "e_sc"}

        identb = sbt("identb", [128, 128], BF16)
        identf = sbt("identf", [128, 128], F32)
        onescol = sbt("onescol", [128, 1], F32)
        onesrow_f = sbt("onesrow_f", [1, 128], F32)
        onesrow_b = sbt("onesrow_b", [1, 128], BF16)
        zeros128 = sbt("zeros128", [128, 128], F32)
        normw_sb = sbt("normw_sb", [128, KC], F32)
        convw_sb = sbt("convw_sb", [128, 320], F32)
        convb_sb = sbt("convb_sb", [128, 80], F32)
        dtb_sb = sbt("dtb_sb", [128, 1], F32)
        acol = sbt("acol", [128, 1], F32)
        dp_sb = sbt("dp_sb", [128, 64], F32)
        gnw_sb = sbt("gnw_sb", [128, 64], F32)
        flag_sb = sbt("flag_sb", [128, 1], F32)
        indm = sbt("indm", [128, 166], BF16)
        tc_sb = sbt("tc_sb", [128, NC_], F32)
        etc_sb = sbt("etc_sb", [128, NC_], F32)
        eps_col = sbt("eps_col", [128, 1], F32)

        for dst, src in [(normw_sb, normw), (convb_sb, convb), (dtb_sb, dtbias),
                         (acol, alog), (dp_sb, dprep), (gnw_sb, gnwi),
                         (flag_sb, flag)]:
            inc(g.dma_start(out=dst[:], in_=src[:]), "s_gp", 16)
        inc(g.dma_start(out=convw_sb[:].rearrange("p (c k) -> p c k", k=4),
                        in_=convw.rearrange("c p k -> p c k")), "s_gp", 16)
        for tile, val in [(identb, 1.0), (identf, 1.0), (onescol, 1.0),
                          (onesrow_f, 1.0), (onesrow_b, 1.0), (zeros128, 0.0),
                          (eps_col, EPS)]:
            inc(g.memset(tile[:], val), "e_pl")
        wait(g, "e_pl")
        inc(g.affine_select(identb[:], identb[:], [[1, 128]], ALU.is_equal, 0.0,
                            base=0, channel_multiplier=-1), "e_pl")
        inc(g.affine_select(identf[:], identf[:], [[1, 128]], ALU.is_equal, 0.0,
                            base=0, channel_multiplier=-1), "e_pl")
        with nc.sbuf_tensor("iota_r", [128, 166], F32) as iota_r, \
             nc.sbuf_tensor("iota_c", [128, 1], I32) as iota_c, \
             nc.sbuf_tensor("qcol", [128, 1], I32) as qcol, \
             nc.sbuf_tensor("qcolf", [128, 1], F32) as qcolf, \
             nc.sbuf_tensor("indf", [128, 166], F32) as indf:
            inc(g.iota(iota_r[:], [[1, 166]], base=0, channel_multiplier=0,
                       allow_small_or_imprecise_dtypes=True), "e_pl")
            inc(g.iota(iota_c[:], [[0, 1]], base=0, channel_multiplier=1), "e_pl")
            wait(v, "e_pl")
            inc(v.tensor_single_scalar(qcol[:], iota_c[:], 4, ALU.logical_shift_right), "e_dve")
            wait(v, "e_dve")
            inc(v.tensor_copy(qcolf[:], qcol[:]), "e_dve")
            wait(v, "e_dve")
            inc(v.tensor_scalar(indf[:], iota_r[:], qcolf[:, 0:1], 80.0,
                                ALU.subtract, ALU.is_equal), "e_dve")
            wait(v, "e_dve")
            inc(v.tensor_copy(indm[:], indf[:]), "e_dve")
        wait(s, "s_gp")
        inc(s.activation(acol[:], acol[:], AF.Exp), "e_sc")
        wait(v, "e_sc")
        inc(v.tensor_single_scalar(acol[:], acol[:], -1.0, ALU.mult), "e_dve")

        h_real = sbt_s("h_real", [128, KC * T], BF16)
        h_halo = sbt_s("h_halo", [128, KC * HW], BF16)

        with nc.sbuf_tensor("xb0", [128, W_ALL], F32) as xb0, \
             nc.sbuf_tensor("xb1", [128, W_ALL], F32) as xb1, \
             nc.sbuf_tensor("sq0", [128, W_ALL], BF16) as sq0, \
             nc.sbuf_tensor("sq1", [128, W_ALL], BF16) as sq1, \
             nc.sbuf_tensor("onescol_b", [128, 1], BF16) as onescol_b, \
             nc.sbuf_tensor("rrow", [1, W_ALL], F32) as rrow, \
             nc.sbuf_tensor("rsrow", [128, W_ALL], F32) as rsrow, \
             nc.psum_tensor("pstat0", [1, 512], F32) as pstat0, \
             nc.psum_tensor("pstat1", [1, 512], F32) as pstat1, \
             nc.psum_tensor("pstat2", [1, 512], F32) as pstat2, \
             nc.psum_tensor("pstat3", [1, 512], F32) as pstat3, \
             nc.psum_tensor("pstat4", [1, 512], F32) as pstat4, \
             nc.psum_tensor("pbc0", [128, 512], F32) as pbc0, \
             nc.psum_tensor("pbc1", [128, 512], F32) as pbc1:
            inc(g.memset(onescol_b[:], 1.0), "e_pl")
            xb = [xb0, xb1]
            sq = [sq0, sq1]
            x_ld = []
            sq_done = []
            mm_done = []
            REGS = [(0, 512), (512, 512), (1024, 512), (1536, 512), (2048, W_ALL - 1 - 2048)]
            wait(t, "s_gp")
            wait(t, "e_pl")
            for kk in range(KC):
                bi = kk % 2
                if kk >= 2:
                    wait(g, "e_sc", sq_done[kk - 2])
                pn = "s_gpa" if bi == 0 else "s_gpb"
                x_ld.append((pn, inc(g.dma_start(out=xb[bi][:], in_=xin[kk, :, :]), pn, 16)))
                wait(s, *x_ld[kk])
                if kk >= 2:
                    wait(s, "e_pe", mm_done[kk - 2])
                sq_done.append(inc(s.activation(sq[bi][:, 1:W_ALL], xb[bi][:, 1:W_ALL],
                                                AF.Square), "e_sc"))
                wait(t, "e_sc", sq_done[kk])
                pstats = [pstat0, pstat1, pstat2, pstat3, pstat4]
                for ri, (off, wd) in enumerate(REGS):
                    ins = t.matmul(pstats[ri][0:1, 0:wd], onescol_b[:],
                                   sq[bi][:, 1 + off: 1 + off + wd],
                                   start=(kk == 0), stop=(kk == KC - 1))
                mm_done.append(inc(ins, "e_pe"))
            wait(s, "e_pe")
            pstats = [pstat0, pstat1, pstat2, pstat3, pstat4]
            for ri, (off, wd) in enumerate(REGS):
                inc(s.copy(rrow[0:1, 1 + off: 1 + off + wd], pstats[ri][0:1, 0:wd]),
                    "e_sc")
            if debug:
                wait(g, "e_sc")
                inc(g.dma_start(out=dbg[11, 0:1, 0:2304], in_=rrow[0:1, 1:2305]),
                    "s_gp", 16)
                wait(s, "s_gp")
            inc(s.activation(rrow[0:1, 1:W_ALL], rrow[0:1, 1:W_ALL], AF.Ln,
                             bias=eps_col[0:1, 0:1], scale=1.0 / D), "e_sc")
            inc(s.activation(rrow[0:1, 1:W_ALL], rrow[0:1, 1:W_ALL], AF.Exp, scale=-0.5),
                "e_sc")
            pbc = [pbc0, pbc1]
            evd = []
            wait(t, "e_sc")
            for i, (off, wd) in enumerate(REGS):
                pb = pbc[i % 2]
                if i >= 2:
                    wait(t, "e_sc2", evd[i - 2])
                inc(t.matmul(pb[:, 0:wd], onesrow_f[:], rrow[0:1, 1 + off: 1 + off + wd],
                             start=True, stop=True), "e_pe")
                wait(s, "e_pe")
                evd.append(inc(s.copy(rsrow[:, 1 + off: 1 + off + wd], pb[:, 0:wd]),
                               "e_sc2"))
            inc(g.memset(rsrow[:, 0:1], 0.0), "e_pl")
            wait(v, "e_sc2")
            wait(v, "e_pl")
            wait(g, "e_sc", sq_done[KC - 1])
            app_done = []
            for kk in range(KC):
                bi = kk % 2
                if kk >= 2:
                    wait(g, "e_dve2", app_done[kk - 2])
                pn = "s_gpa" if bi == 0 else "s_gpb"
                x_ld.append((pn, inc(g.dma_start(out=xb[bi][:], in_=xin[kk, :, :]), pn, 16)))
                wait(v, *x_ld[-1])
                inc(v.scalar_tensor_tensor(
                    h_halo[:, kk * HW:(kk + 1) * HW], xb[bi][:, 0:HW],
                    normw_sb[:, kk:kk + 1], rsrow[:, 0:HW], ALU.mult, ALU.mult), "e_dve2")
                app_done.append(inc(v.scalar_tensor_tensor(
                    h_real[:, kk * T:(kk + 1) * T], xb[bi][:, HW:W_ALL],
                    normw_sb[:, kk:kk + 1], rsrow[:, HW:W_ALL], ALU.mult, ALU.mult),
                    "e_dve2"))
            v_h_done = V["e_dve2"]

        if nphase < 2:
            zo = sbt("zo", [128, T], F32)
            inc(g.memset(zo[:], 0.0), "s_gp")
            wait(sy, "s_gp")
            for kk in range(KC):
                inc(sy.dma_start(out=out[kk, :, :], in_=zo[:]), "s_sy2", 16)
            if debug:
                wait(g, "e_dve2", v_h_done)
                for i in range(4):
                    inc(g.dma_start(out=dbg[i, :, 0:T],
                                    in_=h_real[:, i * T: i * T + T]), "s_gp", 16)
                wait(g, "s_gp")
            wait(sy, "s_sy2")
            return nc
        wts = [sbt_s("wt0", [128, KC * 128], BF16), sbt_s("wt1", [128, KC * 128], BF16)]
        stgs = [sbt_s("stg0", [128, W_ALL], F32), sbt_s("stg1", [128, W_ALL], F32)]
        cscr = sbt_s("cscr", [128, TA], F32)
        hbcb = sbt_s("hbcb", [128, TA], BF16)
        dtT = sbt_s("dtT", [128, NC_ * 128], F32)
        spls = [sbt_s(f"spl{i}", [128, 128], BF16) for i in range(4)]

        phalo = pst_s("phalo", [128, 260])
        p4_ = phalo[:, 0:PADC]
        p256 = phalo[:, PADC:260]
        pbt = [pst_s(f"pb{i}", [128, 512]) for i in range(4)]
        ptp = [pst_s("ptp0", [128, 128], BF16), pst_s("ptp1", [128, 128], BF16)]
        ptf = pst_s("ptf", [128, 128], F32)

        w_ld = []
        mmv = []
        evv = []
        spl_last = {0: None, 1: None, 2: None, 3: None}
        stg_free = []
        scr_free = [("e_pe", 0)]

        def w1_load(mi):
            pn = "s_gpa" if mi % 2 == 0 else "s_gpb"
            w_ld.append((pn, inc(g.dma_start(
                out=wts[mi % 2][:].rearrange("p (k f) -> p k f", k=KC),
                in_=w1[ORD[mi] if mi < len(ORD) else MPROJ, :]
                .rearrange("(k p f) -> p k f", p=128, k=KC)), pn, 16)))

        wait(g_raw, "e_dve2", v_h_done)
        w1_load(0)
        w1_load(1)

        for mi, ordm in enumerate(ORD if p2n is None else ORD[:p2n]):
            is_hbc = mi < N_HBC
            kind = ("dt" if ordm == 144 else
                    "B" if 128 <= ordm < 136 else
                    "C" if 136 <= ordm < 144 else
                    "hid" if 64 <= ordm < 128 else "gate")
            wait(t, *w_ld[mi])
            if mi == 0:
                wait(t, "e_dve2", v_h_done)
            if mi >= 1:
                wait(t, "e_dve", evv[mi - 1])
            wt = wts[mi % 2]
            groups = ([(p4_, h_halo, 0, PADC, HW), (p256, h_halo, PADC, 256, HW)]
                      if is_hbc else [])
            groups += [(pbt[tt], h_real, tt * 512, 512, T) for tt in range(4)]
            for pbuf, hsrc, off, wd, blk in groups:
                pb_ap = pbuf[:, 0:wd] if not isinstance(pbuf, bass.AP) else pbuf
                for k in range(KC):
                    ins = t.matmul(pb_ap, wt[:, k * 128:(k + 1) * 128],
                                   hsrc[:, k * blk + off: k * blk + off + wd],
                                   start=(k == 0), stop=(k == KC - 1))
                inc(ins, "e_pe")
            mmv.append(V["e_pe"])
            if mi + 2 < len(ORD):
                wait(g, "e_pe", mmv[mi])
                w1_load(mi + 2)
            if mi == 1:
                for sn, sv in stg1_guard:
                    wait(v, sn, sv)
            if mi >= 2:
                sn, sv = stg_free[mi - 2]
                wait(v, sn, sv)
            wait(v, "e_pe", mmv[mi])
            stg = stgs[mi % 2]
            if is_hbc:
                inc(v.tensor_copy(stg[:, 0:PADC], p4_), "e_dve")
                inc(v.tensor_copy(stg[:, PADC:HW], p256), "e_dve")
            for tt in range(4):
                inc(v.tensor_copy(stg[:, HW + tt * 512: HW + (tt + 1) * 512],
                                  pbt[tt][:]), "e_dve")
            evv.append(V["e_dve"])

            if kind == "dt":
                fr = _dt_path(nc, V, SEMS, inc, wait, g, v, s, t, sy,
                              stg, stgs[1], cscr, hbcb, dtT, tc_sb, etc_sb,
                              acol, dtb_sb, identf, ptf, zeros128, d_chl,
                              d_ncat, spls, onescol)
                stg_free.append(fr["stg"])
                scr_free = fr["scr"]
                stg1_guard = fr["scr"]
            elif kind in ("B", "C", "hid"):
                ce = v
                cs = ESEM[id(ce)]
                for sn, sv in scr_free:
                    wait(ce, sn, sv)
                wait(ce, "e_dve", evv[mi])
                cc = ordm - 64
                w0 = convw_sb[:, cc * 4: cc * 4 + 1]
                w1c = convw_sb[:, cc * 4 + 1: cc * 4 + 2]
                w2c = convw_sb[:, cc * 4 + 2: cc * 4 + 3]
                w3c = convw_sb[:, cc * 4 + 3: cc * 4 + 4]
                bcol = convb_sb[:, cc:cc + 1]
                inc(ce.tensor_scalar(cscr[:], stg[:, 1:1 + TA], w0, bcol,
                                     ALU.mult, ALU.add), cs)
                inc(ce.scalar_tensor_tensor(cscr[:], stg[:, 2:2 + TA], w1c, cscr[:],
                                            ALU.mult, ALU.add), cs)
                inc(ce.scalar_tensor_tensor(cscr[:], stg[:, 3:3 + TA], w2c, cscr[:],
                                            ALU.mult, ALU.add), cs)
                tap4 = inc(ce.scalar_tensor_tensor(cscr[:], stg[:, 4:4 + TA], w3c,
                                                   cscr[:], ALU.mult, ALU.add), cs)
                stg_free.append((cs, tap4))
                wait(s, cs, tap4)
                silu_v = inc(s.activation(hbcb[:], cscr[:], AF.Silu), "e_sc")
                scr_free = [("e_sc", silu_v)]
                wait(sy, "e_sc", silu_v)
                if not no_spill:
                    if kind == "hid":
                        hh = ordm - 64
                        for c in range(NC_):
                            inc(sy.dma_start(out=d_hidc[hh * NC_ + c, :, :],
                                             in_=hbcb[:, c * 128:(c + 1) * 128]), "s_sy", 16)
                    else:
                        gi = ordm - 128 if kind == "B" else ordm - 136
                        dst = d_b if kind == "B" else d_c
                        for c in range(NC_):
                            inc(sy.dma_start(out=dst[gi, c, :, :],
                                             in_=hbcb[:, c * 128:(c + 1) * 128]), "s_sy", 16)
                scr_free.append(("s_sy", V["s_sy"]))
                if kind in ("hid", "B") and not no_bt:
                    tev = []
                    spf = []
                    for c in range(NC_):
                        wait(t, "e_sc", silu_v)
                        if c >= 2:
                            wait(t, "e_dve2", tev[c - 2])
                        pt = ptp[c % 2][:]
                        inc(t.transpose(pt, hbcb[:, c * 128:(c + 1) * 128],
                                        identb[:]), "e_pe2")
                        wait(v, "e_pe2")
                        spi = (c % 2) if kind == "B" else (2 + c % 2)
                        if spl_last[spi] is not None:
                            wait(v, *spl_last[spi])
                        sp = spls[spi]
                        if kind == "hid":
                            hh = ordm - 64
                            inc(v.tensor_scalar(sp[:, 0:64], pt[:, 0:64],
                                                dtT[:, c * 128 + 2 * hh: c * 128 + 2 * hh + 1],
                                                None, ALU.mult), "e_dve2")
                            tev.append(inc(v.tensor_scalar(
                                sp[:, 64:128], pt[:, 64:128],
                                dtT[:, c * 128 + 2 * hh + 1: c * 128 + 2 * hh + 2],
                                None, ALU.mult), "e_dve2"))
                        else:
                            tev.append(inc(v.tensor_copy(sp[:], pt), "e_dve2"))
                        wait(s, "e_dve2", tev[c])
                        dstt = (d_hidt[(ordm - 64) * NC_ + c, :, :] if kind == "hid"
                                else d_bt[ordm - 128, c, :, :])
                        psn = "s_sca" if c % 2 == 0 else "s_scb"
                        spf.append((psn, inc(s.dma_start(out=dstt, in_=sp[:]), psn, 16)))
                        spl_last[spi] = spf[-1]
                    scr_free.append(("e_pe2", V["e_pe2"]))
            else:
                for sn, sv in scr_free:
                    wait(s, sn, sv)
                wait(s, "e_dve", evv[mi])
                gg = ordm
                sg_v = inc(s.activation(cscr[:, 0:T], stg[:, HW:W_ALL], AF.Silu), "e_sc")
                stg_free.append(("e_sc", sg_v))
                wait(sy, "e_sc", sg_v)
                for c in range(NCR):
                    inc(sy.dma_start(out=d_sg[gg, c, :, :],
                                     in_=cscr[:, c * 128:(c + 1) * 128]), "s_sy", 16)
                scr_free = [("s_sy", V["s_sy"]), ("e_sc", sg_v)]

        p2_done = dict(sy=V["s_sy"], sc=V["s_sc"], dve=V["e_dve"],
                       pe2=V["e_pe2"], esc=V["e_sc"])

        if debug:
            wait(g, "s_sy", p2_done["sy"])
            wait(g, "s_sc", p2_done["sc"])
            wait(g, "s_sca")
            wait(g, "s_scb")
            wait(g, "e_dve", p2_done["dve"])
            for i in range(4):
                inc(g.dma_start(out=dbg[i, :, 0:T],
                                in_=h_real[:, i * T: i * T + T]), "s_gp", 16)
            inc(g.dma_start(out=dbg[4, :, 0:NC_ * 128], in_=dtT[:]), "s_gp", 16)
            inc(g.dma_start(out=dbg[5, :, 0:NC_ * 128], in_=d_ncat[:]), "s_gp", 16)
            inc(g.dma_start(out=dbg[6, :, 0:NC_], in_=tc_sb[:]), "s_gp", 16)
            inc(g.dma_start(out=dbg[6, :, 32:32 + NC_], in_=etc_sb[:]), "s_gp", 16)
            inc(g.dma_start(out=dbg[7, :, 0:128], in_=d_hidc[0 * NC_ + 2, :, :]), "s_gp", 16)
            inc(g.dma_start(out=dbg[7, :, 128:256], in_=d_b[0, 2, :, :]), "s_gp", 16)
            inc(g.dma_start(out=dbg[7, :, 256:384], in_=d_c[0, 2, :, :]), "s_gp", 16)
            inc(g.dma_start(out=dbg[7, :, 384:512], in_=d_hidt[0 * NC_ + 2, :, :]), "s_gp", 16)
            inc(g.dma_start(out=dbg[7, :, 512:640], in_=d_bt[0, 2, :, :]), "s_gp", 16)
            inc(g.dma_start(out=dbg[8, :, 0:TA], in_=d_chl[0, :, :]), "s_gp", 16)
            inc(g.dma_start(out=dbg[9, :, 0:TA], in_=d_chl[1, :, :]), "s_gp", 16)
            inc(g.dma_start(out=dbg[10, :, 0:128], in_=d_sg[0, 0, :, :]), "s_gp", 16)
            inc(g.dma_start(out=dbg[10, :, 128:256], in_=d_hidc[63 * NC_ + 17, :, :]), "s_gp", 16)
            wait(g, "s_gp")

        if nphase < 3:
            zout = stgs[0][:, 0:T]
            wait(g, "e_sc")
            wait(g, "e_dve")
            inc(g.memset(zout, 0.0), "e_pl")
            wait(sy, "e_pl")
            for nm in ["s_gp", "s_gpa", "s_gpb", "s_sy", "s_sc", "s_sca", "s_scb"]:
                wait(sy, nm)
            for kk in range(KC):
                inc(sy.dma_start(out=out[kk, :, :], in_=zout), "s_sy2", 16)
            wait(sy, "s_sy2")
            free_scope()
            return nc

        wait(g_raw, "e_dve2", V["e_dve2"])
        wait(g_raw, "e_dve", V["e_dve"])
        wait(g_raw, "e_pe", V["e_pe"])
        wait(g_raw, "e_sc", V["e_sc"])
        wait(g_raw, "e_pe2", V["e_pe2"])
        wait(g_raw, "s_sy", V["s_sy"])
        wait(g_raw, "s_sca", V["s_sca"])
        wait(g_raw, "s_scb", V["s_scb"])
        barrier_pl = inc(g_raw.engine_nop(), "e_pl")
        for eng in (t, v_raw, s_raw, sy):
            eng.wait_ge(SEMS["e_pl"], barrier_pl)
        free_scope()

        _emit_p3(nc, V, SEMS, inc, wait, g, v, s, t, sy, g_raw, v_raw, s_raw,
                 sbt_s, pst_s, SB=dict(
                     onescol=onescol, onesrow_f=onesrow_f, onesrow_b=onesrow_b,
                     tc_sb=tc_sb, etc_sb=etc_sb, eps_col=eps_col,
                     dp_sb=dp_sb, gnw_sb=gnw_sb, flag_sb=flag_sb),
                 DR=dict(d_hidt=d_hidt, d_hidc=d_hidc, d_b=d_b, d_bt=d_bt,
                         d_c=d_c, d_chl=d_chl, d_ncat=d_ncat, d_sg=d_sg,
                         d_yn=d_yn, d_bnc=d_bnc), dbg=dbg, debug=debug, p3n=p3n)
        for nm in ["e_pe", "e_dve", "e_sc", "e_sc2", "e_dve2", "e_pe2",
                   "s_sy", "s_yn", "s_cc", "s_h0", "s_h1", "s_c0", "s_c1",
                   "s_m0", "s_m1", "s_g0", "s_g1"]:
            wait(g_raw, nm, V[nm])
        bar3 = inc(g_raw.engine_nop(), "e_pl")
        for eng in (t, v_raw, s_raw, sy):
            eng.wait_ge(SEMS["e_pl"], bar3)
        free_scope()

        if nphase >= 4:
            _emit_p45(nc, V, SEMS, inc, wait, g, v, s, t, sy, g_raw, v_raw,
                      s_raw, sbt_s, pst_s, (free_scope, mark_scope, free_to),
                      SB=dict(onescol=onescol, onesrow_f=onesrow_f,
                              onesrow_b=onesrow_b, eps_col=eps_col, indm=indm),
                      DR=dict(d_yn=d_yn, d_x1=d_x1, d_y1=d_y1, d_int=d_int,
                              d_wd=d_wd, d_x2=d_x2, d_bnc=d_bnc),
                      IO=dict(xin=xin, w2=w2, out=out, gq=gq, gsr=gsr, gz=gz,
                              gso=gso, uq=uq, usr=usr, uz=uz, uso=uso, dq=dq,
                              dsr=dsr, dz=dz, dso=dso),
                      dbg=dbg, debug=debug)
            return nc

        if nphase < 4:
            zo2 = sbt_s("zo2", [128, T], F32)
            wait(g, "e_dve")
            wait(g, "e_sc")
            inc(g.memset(zo2[:], 0.0), "e_pl")
            wait(sy, "e_pl")
            for nm in ["s_gp", "s_gpa", "s_gpb", "s_sy", "s_sc", "s_sca",
                       "s_scb", "s_h0", "s_h1", "s_c0", "s_c1", "s_m0",
                       "s_m1", "s_g0", "s_g1", "s_cc", "s_yn"]:
                wait(sy, nm)
            for kk in range(KC):
                inc(sy.dma_start(out=out[kk, :, :], in_=zo2[:]), "s_sy2", 16)
            wait(sy, "s_sy2")
            free_scope()
            return nc

    return nc


def _emit_p3(nc, V, SEMS, inc, wait, g, v, s, t, sy, g_raw, v_raw, s_raw,
             sbt_s, pst_s, SB, DR, dbg, debug, p3n=64):
    onescol = SB["onescol"]; onesrow_f = SB["onesrow_f"]
    onesrow_b = SB["onesrow_b"]
    tc_sb = SB["tc_sb"]; etc_sb = SB["etc_sb"]; eps_col = SB["eps_col"]
    dp_sb = SB["dp_sb"]; gnw_sb = SB["gnw_sb"]; flag_sb = SB["flag_sb"]
    d_hidt = DR["d_hidt"]; d_hidc = DR["d_hidc"]; d_b = DR["d_b"]
    d_bt = DR["d_bt"]; d_c = DR["d_c"]; d_chl = DR["d_chl"]
    d_ncat = DR["d_ncat"]; d_sg = DR["d_sg"]; d_yn = DR["d_yn"]; d_bnc = DR["d_bnc"]

    y_raw = sbt_s("y_raw", [128, 8 * T], F32)
    ncat_sb = sbt_s("ncat_sb", [128, NC_ * 128], F32)
    TCB = sbt_s("TCB", [128, 128 * NC_], F32)
    EXB = sbt_s("EXB", [128, 128 * NC_], F32)
    RB = [sbt_s("RB0", [128, TA], F32), sbt_s("RB1", [128, TA], F32)]
    _EXR = sbt_s("EXR", [128, TA], BF16)
    EXPROW = [_EXR, _EXR]
    CEXP = [sbt_s("CEX0", [128, T], BF16), sbt_s("CEX1", [128, T], BF16)]
    MT = [sbt_s("MT0", [128, T], BF16), sbt_s("MT1", [128, T], BF16)]
    DIFF = sbt_s("DIFF", [128, T // 2], F32)
    bcache = sbt_s("bcache", [128, NC_ * 128], BF16)
    ccache = sbt_s("ccache", [128, NC_ * 128], BF16)
    btcache = sbt_s("btcache", [128, NC_ * 128], BF16)
    gtm = sbt_s("gtm", [128, NCR * 128], BF16)
    ht_t = [sbt_s("ht0", [128, 128], BF16), sbt_s("ht1", [128, 128], BF16)]
    hc_t = [sbt_s("hc0", [128, 128], BF16), sbt_s("hc1", [128, 128], BF16)]
    hwb = sbt_s("hwb", [128, 128], BF16)
    _sgb = sbt_s("sgb0", [128, T], F32)
    sgb = [_sgb, _sgb]
    S_tot = sbt_s("S_tot", [128, 128], F32)
    S_bf = sbt_s("S_bf", [128, 128], BF16)
    EW = [sbt_s("EW0", [128, NC_], F32), sbt_s("EW1", [128, NC_], F32)]
    rowAB = [sbt_s("rowA", [1, 2 * TA], BF16), sbt_s("rowB", [1, 2 * TA], BF16)]
    grow = sbt_s("grow", [1, 2304], F32)
    GROW = sbt_s("GROW", [128, T], F32)
    sqb = sbt_s("sqb", [128, T], BF16)
    ynb = sqb
    onescol_b2 = sbt_s("onescol_b2", [128, 1], BF16)

    pyb = [pst_s("pyb0", [128, 512]), pst_s("pyb1", [128, 512])]
    pstt = [pst_s("pstt0", [128, 512]), pst_s("pstt1", [128, 512])]
    pgt = pst_s("pgt", [128, 512])
    prb = [pst_s("prb0", [128, 512]), pst_s("prb1", [128, 512])]
    pgn = pst_s("pgn", [128, 512])

    inc(g.memset(onescol_b2[:], 1.0), "e_pl")
    pbank_free = {}

    def bwait(eng, key):
        if key in pbank_free:
            sn, sv = pbank_free[key]
            wait(eng, sn, sv)

    def bset(key, sn):
        pbank_free[key] = (sn, V[sn])
    inc(sy.dma_start(out=ncat_sb[:], in_=d_ncat[:, :]), "s_sy", 16)
    inc(sy.dma_start(out=d_bnc[2, 0:128 * NC_].rearrange("(p c) -> p c", c=NC_),
                     in_=tc_sb[:]), "s_sy", 16)
    inc(sy.dma_start(out=d_bnc[3, 0:128 * NC_].rearrange("(p c) -> p c", c=NC_),
                     in_=etc_sb[:]), "s_sy", 16)
    wait(sy, "s_sy")
    evd2 = []
    for ti, (tbl, rowi) in enumerate(((TCB, 2), (EXB, 3))):
        if ti > 0:
            wait(sy, "e_pe")
        inc(sy.dma_start(out=grow[0:1, 0:2304], in_=d_bnc[rowi:rowi + 1, 0:2304]),
            "s_sy", 16)
        wait(t, "s_sy")
        for i, (off, wd) in enumerate([(0, 512), (512, 512), (1024, 512),
                                       (1536, 512), (2048, 256)]):
            pb = prb[i % 2]
            bwait(t, f"prb{i % 2}")
            inc(t.matmul(pb[:, 0:wd], onesrow_f[:], grow[0:1, off:off + wd],
                         start=True, stop=True), "e_pe")
            wait(s, "e_pe")
            evd2.append(inc(s.copy(tbl[:, off:off + wd], pb[:, 0:wd]), "e_sc2"))
            bset(f"prb{i % 2}", "e_sc2")
    wait(v, "e_sc2")

    py_evac = [0, 0]
    st_upd = [0, 0]
    stmm = [0, 0]
    ht_cons = [(0, 0), (0, 0)]
    hc_cons = [0, 0]
    rb_rows_cons = [0, 0]
    diff_done = [0, 0]
    ymm_done = [0, 0]
    sg_cons = [0, 0]
    gtm_ready = 0
    cache_ld = 0

    for hh in range(p3n):
        gr = hh // 8
        hhl = hh % 8
        if hhl == 0:
            if gr > 0:
                wait(sy, "e_pe")
                wait(sy, "e_dve")
            for cch, src in ((bcache, d_b), (ccache, d_c), (btcache, d_bt)):
                inc(sy.dma_start(out=cch[:].rearrange("p (c f) -> p c f", c=NC_),
                                 in_=src[gr, :, :, :].rearrange("c p f -> p c f")),
                    "s_cc", 16)
            cache_ld = V["s_cc"]
            wait(t, "s_cc", cache_ld)
            gt_ev = []
            for ci in range(NCR):
                c = ci + 2
                if ci >= 1:
                    wait(t, "e_dve", gt_ev[ci - 1])
                bwait(t, "pgt")
                inc(t.matmul(pgt[:, 0:128], bcache[:, c * 128:(c + 1) * 128],
                             ccache[:, c * 128:(c + 1) * 128],
                             start=True, stop=True), "e_pe")
                wait(v, "e_pe")
                gt_ev.append(inc(v.tensor_copy(gtm[:, ci * 128:(ci + 1) * 128],
                                               pgt[:, 0:128]), "e_dve"))
                bset("pgt", "e_dve")
            wait(g, "e_dve", gt_ev[-1])
            for ci in range(NCR):
                inc(g.affine_select(gtm[:, ci * 128:(ci + 1) * 128],
                                    gtm[:, ci * 128:(ci + 1) * 128],
                                    [[1, 128]], ALU.is_ge, 0.0,
                                    base=0, channel_multiplier=-1), "e_pl")
            gtm_ready = V["e_pl"]

        for hi in range(2):
            h = 2 * hh + hi
            pn = "s_m0" if hi == 0 else "s_m1"
            wait(sy, "e_pe", rb_rows_cons[hi])
            rw = rowAB[hi]
            if hh == 0 and hi == 0:
                wait(t, "e_pl")
            inc(sy.dma_start(out=rw[0:1, 0:TA], in_=d_chl[0, h:h + 1, :]), pn, 16)
            inc(sy.dma_start(out=rw[0:1, TA:2 * TA], in_=d_chl[1, h:h + 1, :]), pn, 16)
            rowv = V[pn]
            wait(t, pn, rowv)
            wait(t, "e_sc", diff_done[hi])
            wait(t, "e_dve", max(V["e_dve"] if False else 0, 0))
            for i, (off, wd) in enumerate([(0, 512), (512, 512), (1024, 512),
                                           (1536, 512), (2048, 256)]):
                pb = prb[i % 2]
                bwait(t, f"prb{i % 2}")
                inc(t.matmul(pb[:, 0:wd], onesrow_b[:],
                             rw[0:1, off: off + wd],
                             start=True, stop=False), "e_pe")
                inc(t.matmul(pb[:, 0:wd], onesrow_b[:],
                             rw[0:1, TA + off: TA + off + wd],
                             start=False, stop=True), "e_pe")
                wait(s, "e_pe")
                if i == 0:
                    wait(s, "e_dve", V["e_dve"])
                inc(s.copy(RB[hi][:, off:off + wd], pb[:, 0:wd]), "e_sc2")
                bset(f"prb{i % 2}", "e_sc2")
            rb_rows_cons[hi] = V["e_pe"]
            if hi == 1:
                wait(s, "e_dve", V["e_dve"])
            exprow_v = inc(s.activation(EXPROW[hi][:], RB[hi][:], AF.Exp), "e_sc")
            wait(v, "e_sc", exprow_v)
            wait(v, "s_cc", cache_ld)
            wait(v, "e_pe", ymm_done[hi])
            inc(v.tensor_tensor(CEXP[hi][:], ccache[:, 2 * 128: NC_ * 128],
                                EXPROW[hi][:, 2 * 128: NC_ * 128], ALU.mult), "e_dve")
            inc(v.tensor_add(EW[hi][:], TCB[:, h * NC_:(h + 1) * NC_],
                             ncat_sb[:, h: (NC_ - 1) * 128 + h + 1: 128]), "e_dve")
            ew_add = V["e_dve"]
            wait(s, "e_dve", ew_add)
            inc(s.activation(EW[hi][:], EW[hi][:], AF.Exp), "e_sc")
            for half in range(2):
                wait(v, "e_sc", diff_done[hi])
                for ci in range(half * 8, half * 8 + 8):
                    c = ci + 2
                    inc(v.tensor_scalar(DIFF[:, (ci - half * 8) * 128:(ci - half * 8 + 1) * 128],
                                        RB[hi][:, c * 128:(c + 1) * 128],
                                        ncat_sb[:, c * 128 + h: c * 128 + h + 1],
                                        0.0, ALU.add, ALU.min), "e_dve")
                dv = V["e_dve"]
                wait(s, "e_dve", dv)
                diff_done[hi] = inc(s.activation(MT[hi][:, half * 1024:(half + 1) * 1024],
                                                 DIFF[:], AF.Exp), "e_sc")
            wait(v, "e_sc", diff_done[hi])
            wait(v, "e_pl", gtm_ready)
            inc(v.tensor_tensor(MT[hi][:], MT[hi][:], gtm[:, 0:NCR * 128],
                                ALU.mult), "e_dve")
        mt_cex_ready = V["e_dve"]
        ew_ready = V["e_sc"]

        wait(v, "e_dve")
        inc(v.memset(S_tot[:], 0.0), "e_dve")
        for c in range(NC_):
            p2 = c % 2
            hsem = "s_h0" if p2 == 0 else "s_h1"
            wait(sy, "e_pe", ht_cons[p2][0])
            wait(sy, "e_dve", ht_cons[p2][1])
            inc(sy.dma_start(out=ht_t[p2][:], in_=d_hidt[hh * NC_ + c, :, :]),
                hsem, 16)
            ht_v = V[hsem]
            if c >= 2:
                csem = "s_c0" if p2 == 0 else "s_c1"
                wait(sy, "e_dve", hc_cons[p2])
                inc(sy.dma_start(out=hc_t[p2][:], in_=d_hidc[hh * NC_ + c, :, :]),
                    csem, 16)
                hc_v = V[csem]
            if c >= 2:
                for hi in range(2):
                    inc(v.tensor_copy(S_bf[:, hi * 64:(hi + 1) * 64],
                                      S_tot[:, hi * 64:(hi + 1) * 64]), "e_dve")
                sbf_v = V["e_dve"]
                wait(t, "e_dve", sbf_v)
                wait(t, "e_dve", py_evac[p2] if False else max(py_evac[p2], mt_cex_ready))
                wait(t, hsem, ht_v)
                for hi in range(2):
                    inc(t.matmul(pyb[p2][hi * 64:(hi + 1) * 64, 0:128],
                                 S_bf[:, hi * 64:(hi + 1) * 64],
                                 CEXP[hi][:, (c - 2) * 128:(c - 1) * 128],
                                 start=True, stop=False,
                                 tile_position=(0, 64 * hi)), "e_pe")
                    inc(t.matmul(pyb[p2][hi * 64:(hi + 1) * 64, 0:128],
                                 ht_t[p2][:, hi * 64:(hi + 1) * 64],
                                 MT[hi][:, (c - 2) * 128:(c - 1) * 128],
                                 start=False, stop=True,
                                 tile_position=(0, 64 * hi)), "e_pe")
            ymm_c = V["e_pe"]
            wait(v, hsem, ht_v)
            wait(v, "e_sc", ew_ready)
            wait(v, "e_pe", stmm[1 - p2])
            for hi in range(2):
                h = 2 * hh + hi
                inc(v.tensor_scalar(hwb[:, hi * 64:(hi + 1) * 64],
                                    ht_t[p2][:, hi * 64:(hi + 1) * 64],
                                    EW[hi][:, c:c + 1], None, ALU.mult), "e_dve")
            hw_v = V["e_dve"]
            wait(t, "e_dve", max(hw_v, st_upd[p2]))
            stmm[p2] = inc(t.matmul(pstt[p2][:, 0:128],
                                    btcache[:, c * 128:(c + 1) * 128],
                                    hwb[:], start=True, stop=True), "e_pe")
            wait(v, "e_pe", stmm[p2])
            for hi in range(2):
                h = 2 * hh + hi
                inc(v.scalar_tensor_tensor(
                    S_tot[:, hi * 64:(hi + 1) * 64], S_tot[:, hi * 64:(hi + 1) * 64],
                    EXB[:, h * NC_ + c: h * NC_ + c + 1],
                    pstt[p2][:, hi * 64:(hi + 1) * 64], ALU.mult, ALU.add), "e_dve")
            st_upd[p2] = V["e_dve"]
            if c == 1:
                inc(v.tensor_scalar(S_tot[:], S_tot[:], flag_sb[:, 0:1], None,
                                    ALU.mult), "e_dve")
                st_upd[p2] = V["e_dve"]
            if c >= 2:
                wait(v, "e_pe", ymm_c)
                wait(v, csem, hc_v)
                inc(v.scalar_tensor_tensor(
                    y_raw[:, hhl * T + (c - 2) * 128: hhl * T + (c - 1) * 128],
                    hc_t[p2][:], dp_sb[:, hh:hh + 1], pyb[p2][0:128, 0:128],
                    ALU.mult, ALU.add), "e_dve")
                py_evac[p2] = V["e_dve"]
                hc_cons[p2] = V["e_dve"]
            ht_cons[p2] = (ymm_c, hw_v)
        ymm_done[0] = ymm_done[1] = V["e_pe"]

        if hhl == 7:
            if debug and gr == 0:
                wait(g, "e_dve")
                inc(g.dma_start(out=dbg[12, :, 0:T], in_=y_raw[:, 0:T]), "s_gp", 16)
                wait(v_raw, "s_gp", V["s_gp"])
            for h2 in range(8):
                gg = 8 * gr + h2
                wait(sy, "e_dve", sg_cons[0])
                inc(sy.dma_start(out=sgb[0][:].rearrange("p (c f) -> p c f", c=NCR),
                                 in_=d_sg[gg, :, :, :].rearrange("c p f -> p c f")),
                    "s_g0", 16)
                sg_v = V["s_g0"]
                wait(v, "s_g0", sg_v)
                inc(v.tensor_tensor(y_raw[:, h2 * T:(h2 + 1) * T],
                                    y_raw[:, h2 * T:(h2 + 1) * T],
                                    sgb[0][:], ALU.mult), "e_dve")
                sg_cons[0] = V["e_dve"]
                wait(s, "e_dve", sg_cons[0])
                if h2 > 0:
                    wait(s, "e_pe", V["e_pe"])
                else:
                    wait(s, "s_yn", V["s_yn"])
                inc(s.activation(sqb[:], y_raw[:, h2 * T:(h2 + 1) * T], AF.Square),
                    "e_sc")
                wait(t, "e_sc", V["e_sc"])
                banks = [pgn, prb[0], prb[1], pgt]
                bnames = ["pgn", "prb0", "prb1", "pgt"]
                for ri in range(4):
                    if h2 == 0:
                        bwait(t, bnames[ri])
                    inc(t.matmul(banks[ri][0:1, 0:512], onescol_b2[:],
                                 sqb[:, ri * 512:(ri + 1) * 512],
                                 start=(h2 == 0), stop=(h2 == 7)), "e_pe")
            wait(s, "e_pe")
            banks = [pgn, prb[0], prb[1], pgt]
            bnames = ["pgn", "prb0", "prb1", "pgt"]
            for ri in range(4):
                inc(s.copy(grow[0:1, ri * 512:(ri + 1) * 512],
                           banks[ri][0:1, 0:512]), "e_sc")
                bset(bnames[ri], "e_sc")
            inc(s.activation(grow[0:1, 0:T], grow[0:1, 0:T], AF.Ln,
                             bias=eps_col[0:1, 0:1], scale=1.0 / 1024), "e_sc")
            inc(s.activation(grow[0:1, 0:T], grow[0:1, 0:T], AF.Exp, scale=-0.5),
                "e_sc")
            grv = V["e_sc"]
            wait(t, "e_sc", grv)
            for ri in range(4):
                pb = prb[ri % 2]
                bwait(t, f"prb{ri % 2}")
                inc(t.matmul(pb[:, 0:512], onesrow_f[:],
                             grow[0:1, ri * 512:(ri + 1) * 512],
                             start=True, stop=True), "e_pe")
                wait(s, "e_pe")
                inc(s.copy(GROW[:, ri * 512:(ri + 1) * 512], pb[:, 0:512]), "e_sc2")
                bset(f"prb{ri % 2}", "e_sc2")
            wait(v, "e_sc2")
            yn_sp = 0
            for h2 in range(8):
                gg = 8 * gr + h2
                if h2 > 0:
                    wait(v, "s_yn", yn_sp)
                inc(v.scalar_tensor_tensor(ynb[:], y_raw[:, h2 * T:(h2 + 1) * T],
                                           gnw_sb[:, gg:gg + 1], GROW[:],
                                           ALU.mult, ALU.mult), "e_dve")
                wait(sy, "e_dve")
                inc(sy.dma_start(out=d_yn[gg, :, :], in_=ynb[:]), "s_yn", 16)
                yn_sp = V["s_yn"]
            if debug and gr == 0:
                wait(g, "s_yn", yn_sp)
                inc(g.dma_start(out=dbg[13, :, 0:T], in_=d_yn[0, :, :]), "s_gp", 16)


def _emit_p45(nc, V, SEMS, inc, wait, g, v, s, t, sy, g_raw, v_raw, s_raw,
              sbt_s, pst_s, scope_fns, SB, DR, IO, dbg, debug):
    free_scope, mark_scope, free_to = scope_fns
    onescol = SB["onescol"]; onesrow_f = SB["onesrow_f"]
    onesrow_b = SB["onesrow_b"]; eps_col = SB["eps_col"]; indm = SB["indm"]
    d_yn = DR["d_yn"]; d_x1 = DR["d_x1"]; d_y1 = DR["d_y1"]
    d_bnc = DR["d_bnc"]
    xin = IO["xin"]; w2 = IO["w2"]

    Xs = sbt_s("Xs", [32, T], BF16)
    Xsd = sbt_s("Xsd", [86, T], BF16)
    nzd = sbt_s("nzd", [86, D], BF16)
    mk_base = mark_scope()

    def p4a():
        ynsl = sbt_s("ynsl", [128, 64 * 512], BF16)
        w2t = [sbt_s("w2t0", [128, 64 * 128], BF16),
               sbt_s("w2t1", [128, 64 * 128], BF16)]
        xres = [sbt_s("xres0", [128, 512], F32), sbt_s("xres1", [128, 512], F32)]
        x1b = [sbt_s("x1b0", [128, 512], F32), sbt_s("x1b1", [128, 512], F32)]
        sqx = sbt_s("sqx", [128, 512], BF16)
        oneb = sbt_s("oneb", [128, 1], BF16)
        stmp = sbt_s("stmp", [1, 1024], F32)
        pmm = [pst_s("pm40", [128, 512]), pst_s("pm41", [128, 512])]
        psm = pst_s("psm", [128, 512])
        psq = pst_s("psq", [128, 512])
        inc(g.memset(oneb[:], 1.0), "e_pl")
        w_ld = [None, None]
        mm_done = [0, 0]
        ev_done = [0, 0]
        x1sp = [None, None]
        stat_mm = 0
        stat_ev = 0

        def w2_load(it):
            pn2 = "s_gpa" if it % 2 == 0 else "s_gpb"
            inc(g.dma_start(
                out=w2t[it % 2][:].rearrange("p (k f) -> p k f", k=64),
                in_=w2[it % KC, :].rearrange("(k p f) -> p k f", p=128, k=64)),
                pn2, 16)
            w_ld[it % 2] = (pn2, V[pn2])

        w2_load(0)
        w2_load(1)
        for tt in range(4):
            wait(sy, "e_pe")
            for kk2 in range(64):
                inc(sy.dma_start(out=ynsl[:, kk2 * 512:(kk2 + 1) * 512],
                                 in_=d_yn[kk2, :, tt * 512:(tt + 1) * 512]),
                    "s_cc", 16)
            slab_v = V["s_cc"]
            for m in range(KC):
                it = tt * KC + m
                bi = it % 2
                pxn = "s_m0" if bi == 0 else "s_m1"
                wait(sy, "e_dve", ev_done[bi])
                inc(sy.dma_start(out=xres[bi][:],
                                 in_=xin[m, :, HW + tt * 512: HW + (tt + 1) * 512]),
                    pxn, 16)
                x_ld = (pxn, V[pxn])
                wait(t, *w_ld[bi])
                wait(t, "s_cc", slab_v)
                wait(t, "e_dve", ev_done[bi])
                for k in range(64):
                    ins = t.matmul(pmm[bi][:, 0:512],
                                   w2t[bi][:, k * 128:(k + 1) * 128],
                                   ynsl[:, k * 512:(k + 1) * 512],
                                   start=(k == 0), stop=(k == 63))
                mm_done[bi] = inc(ins, "e_pe")
                if it + 2 < 128:
                    wait(g, "e_pe", mm_done[bi])
                    w2_load(it + 2)
                wait(v, "e_pe", mm_done[bi])
                wait(v, *x_ld)
                if x1sp[bi] is not None:
                    wait(v, *x1sp[bi])
                wait(v, "e_pe", stat_mm)
                inc(v.scalar_tensor_tensor(x1b[bi][:], xres[bi][:], 2.0,
                                           pmm[bi][:, 0:512], ALU.mult, ALU.add),
                    "e_dve")
                ev_done[bi] = V["e_dve"]
                wait(sy, "e_dve", ev_done[bi])
                psp = "s_h0" if bi == 0 else "s_h1"
                inc(sy.dma_start(out=d_x1[m, :, tt * 512:(tt + 1) * 512],
                                 in_=x1b[bi][:]), psp, 16)
                x1sp[bi] = (psp, V[psp])
                wait(s, "e_dve", ev_done[bi])
                wait(s, "e_pe", stat_mm)
                inc(s.activation(sqx[:], x1b[bi][:], AF.Square), "e_sc")
                sqv = V["e_sc"]
                wait(t, "e_sc", sqv)
                if m == 0:
                    wait(t, "e_sc2", stat_ev)
                t.matmul(psm[0:1, 0:512], onescol[:], x1b[bi][:],
                         start=(m == 0), stop=(m == KC - 1))
                ins = t.matmul(psq[0:1, 0:512], oneb[:], sqx[:],
                               start=(m == 0), stop=(m == KC - 1))
                stat_mm = inc(ins, "e_pe")
            wait(s, "e_pe", stat_mm)
            wait(s, "s_sc", V["s_sc"])
            inc(s.copy(stmp[0:1, 0:512], psm[0:1, 0:512]), "e_sc2")
            inc(s.copy(stmp[0:1, 512:1024], psq[0:1, 0:512]), "e_sc2")
            stat_ev = V["e_sc2"]
            wait(sy, "e_sc2", stat_ev)
            inc(sy.dma_start(out=d_bnc[4, tt * 512:(tt + 1) * 512],
                             in_=stmp[0:1, 0:512]), "s_sc", 16)
            inc(sy.dma_start(out=d_bnc[5, tt * 512:(tt + 1) * 512],
                             in_=stmp[0:1, 512:1024]), "s_sc", 16)

    p4a()
    for nm in ["e_pe", "e_dve", "e_sc", "e_sc2", "s_cc", "s_m0", "s_m1",
               "s_h0", "s_h1", "s_gpa", "s_gpb"]:
        wait(g_raw, nm, V[nm])
    bar = inc(g_raw.engine_nop(), "e_pl")
    for eng in (t, v_raw, s_raw, sy):
        eng.wait_ge(SEMS["e_pl"], bar)
    free_to(mk_base)
    y1res = sbt_s("y1res", [128, KC * T], BF16)
    mk_y1 = mark_scope()

    def p4b():
        MUR = sbt_s("MUR", [128, T], F32)
        RSR = sbt_s("RSR", [128, T], F32)
        mu_row = sbt_s("mu_row", [1, T], F32)
        vrow = sbt_s("vrow", [1, T], F32)
        wait(sy, "s_sc", V["s_sc"])
        inc(sy.dma_start(out=mu_row[0:1, :], in_=d_bnc[4:5, 0:T]), "s_sy", 16)
        inc(sy.dma_start(out=vrow[0:1, :], in_=d_bnc[5:6, 0:T]), "s_sy", 16)
        wait(v, "s_sy", V["s_sy"])
        x1rb = [sbt_s("x1rb0", [128, T], F32), sbt_s("x1rb1", [128, T], F32)]
        pr4 = [pst_s("pr40", [128, 512]), pst_s("pr41", [128, 512])]

        inc(v.tensor_single_scalar(mu_row[0:1, :], mu_row[0:1, :], 1.0 / D,
                                   ALU.mult), "e_dve")
        inc(v.tensor_single_scalar(vrow[0:1, :], vrow[0:1, :], 1.0 / D,
                                   ALU.mult), "e_dve")
        inc(v.tensor_tensor(MUR[0:1, :], mu_row[0:1, :], mu_row[0:1, :],
                            ALU.mult), "e_dve")
        inc(v.tensor_sub(vrow[0:1, :], vrow[0:1, :], MUR[0:1, :]), "e_dve")
        vv = V["e_dve"]
        wait(s, "e_dve", vv)
        inc(s.activation(vrow[0:1, :], vrow[0:1, :], AF.Ln,
                         bias=eps_col[0:1, 0:1]), "e_sc")
        inc(s.activation(vrow[0:1, :], vrow[0:1, :], AF.Exp, scale=-0.5), "e_sc")
        rsv = V["e_sc"]
        wait(t, "e_sc", rsv)
        pb_ev = [0, 0]
        for ri in range(8):
            src, dst = (mu_row, MUR) if ri < 4 else (vrow, RSR)
            off = (ri % 4) * 512
            pb = pr4[ri % 2]
            wait(t, "e_sc2", pb_ev[ri % 2])
            inc(t.matmul(pb[:, 0:512], onesrow_f[:], src[0:1, off:off + 512],
                         start=True, stop=True), "e_pe")
            wait(s, "e_pe")
            inc(s.copy(dst[:, off:off + 512], pb[:, 0:512]), "e_sc2")
            pb_ev[ri % 2] = V["e_sc2"]
        wait(v, "e_sc2")
        x1_ld = [None, None]
        ap_done = [0, 0]
        y1sp = [None, None]
        for m in range(KC):
            bi = m % 2
            pxn = "s_m0" if bi == 0 else "s_m1"
            wait(sy, "e_dve", ap_done[bi])
            if y1sp[bi] is not None:
                wait(sy, *y1sp[bi])
            inc(sy.dma_start(out=x1rb[bi][:], in_=d_x1[m, :, :]), pxn, 16)
            x1_ld[bi] = (pxn, V[pxn])
            wait(v, *x1_ld[bi])
            inc(v.tensor_sub(x1rb[bi][:], x1rb[bi][:], MUR[:]), "e_dve")
            inc(v.tensor_tensor(x1rb[bi][:], x1rb[bi][:], RSR[:], ALU.mult),
                "e_dve")
            inc(v.tensor_copy(y1res[:, m * T:(m + 1) * T], x1rb[bi][:]), "e_dve")
            ap_done[bi] = V["e_dve"]
            wait(sy, "e_dve", ap_done[bi])
            psp = "s_h0" if bi == 0 else "s_h1"
            inc(sy.dma_start(out=d_y1[m, :, :], in_=x1rb[bi][:]), psp, 16)
            y1sp[bi] = (psp, V[psp])

    p4b()
    if debug:
        wait(g, "e_dve", V["e_dve"])
        inc(g.dma_start(out=dbg[14, :, 0:T], in_=y1res[:, 0:T]), "s_gp", 16)
        wait(g_raw, "s_gp", V["s_gp"])
    for nm in ["e_pe", "e_dve", "e_sc", "e_sc2", "s_m0", "s_m1", "s_h0", "s_h1"]:
        wait(g_raw, nm, V[nm])
    bar = inc(g_raw.engine_nop(), "e_pl")
    for eng in (t, v_raw, s_raw, sy):
        eng.wait_ge(SEMS["e_pl"], bar)
    free_to(mk_y1)
    _emit_p5(nc, V, SEMS, inc, wait, g, v, s, t, sy, g_raw, v_raw, s_raw,
             sbt_s, pst_s, (free_scope, mark_scope, free_to, mk_base, mk_y1),
             y1res, Xs, Xsd, nzd, SB, DR, IO, dbg, debug)


def _emit_p5(nc, V, SEMS, inc, wait, g, v, s, t, sy, g_raw, v_raw, s_raw,
             sbt_s, pst_s, scope_fns, y1res, Xs, Xsd, nzd, SB, DR, IO, dbg,
             debug):
    free_scope, mark_scope, free_to, mk_base, mk_y1 = scope_fns
    onescol = SB["onescol"]; onesrow_f = SB["onesrow_f"]
    onesrow_b = SB["onesrow_b"]; eps_col = SB["eps_col"]; indm = SB["indm"]
    d_int = DR["d_int"]; d_wd = DR["d_wd"]; d_x2 = DR["d_x2"]; d_y1 = DR["d_y1"]
    out = IO["out"]
    gq = IO["gq"]; gsr = IO["gsr"]; gz = IO["gz"]; gso = IO["gso"]
    uq = IO["uq"]; usr = IO["usr"]; uz = IO["uz"]; uso = IO["uso"]
    dq = IO["dq"]; dsr = IO["dsr"]; dz = IO["dz"]; dso = IO["dso"]

    d_nzp = nc.dram_tensor("d_nzp", [2, 32, INTP], BF16, kind="Internal").ap()

    def p5a():
        qb = [sbt_s("qb0", [128, 512], I32), sbt_s("qb1", [128, 512], I32)]
        sb_ = [sbt_s("sb0", [128, 512], F16), sbt_s("sb1", [128, 512], F16)]
        wgu = [sbt_s("wg0", [128, KC * 128], BF16),
               sbt_s("wg1", [128, KC * 128], BF16)]
        uscr = sbt_s("uscr", [128, 512], I32)
        sgate = sbt_s("sgate", [128, 512], F32)
        intb = [sbt_s("intb0", [128, 512], BF16), sbt_s("intb1", [128, 512], BF16)]
        nzt = [sbt_s("nzt0", [32, 128], BF16), sbt_s("nzt1", [32, 128], BF16)]
        zint = sbt_s("zint", [86, 1408], I32)
        zf = sbt_s("zf", [86, 1408], F32)
        zsc = sbt_s("zsc", [86, 1408], F16)
        zbf = sbt_s("zbf", [86, 1408], BF16)
        pgu = [pst_s("pgu0", [128, 512]), pst_s("pgu1", [128, 512])]
        pxs = [pst_s(f"pxs{i}", [128, 512]) for i in range(4)]

        zw = 0
        for side, (zz, sso) in enumerate(((gz, gso), (uz, uso))):
            wait(sy, "e_dve", zw)
            inc(sy.dma_start(out=zint[0:32, 0:1376], in_=zz[:, :]), "s_sy", 16)
            for j in range(8):
                wait(sy, "e_dve", zw)
                inc(sy.dma_start(out=zsc[0:32, 0:1376],
                                 in_=sso[:, j * 1408: j * 1408 + 1376]), "s_sy", 16)
                wait(v, "s_sy")
                inc(v.tensor_scalar(zf[0:32, 0:1376].bitcast(I32), zint[0:32, 0:1376],
                                    4 * j, 15, ALU.logical_shift_right,
                                    ALU.bitwise_and), "e_dve")
                inc(v.tensor_scalar(zf[0:32, 0:1376], zf[0:32, 0:1376].bitcast(I32),
                                    -1.0, -1.0, ALU.mult, ALU.add), "e_dve")
                inc(v.tensor_tensor(zbf[0:32, 0:1376], zf[0:32, 0:1376],
                                    zsc[0:32, 0:1376], ALU.mult), "e_dve")
                zw = V["e_dve"]
                wait(sy, "e_dve", zw)
                inc(sy.dma_start(out=d_nzp[side, :, j * 1408: j * 1408 + 1376],
                                 in_=zbf[0:32, 0:1376]), "s_sy", 16)
        wait(v, "s_sy")
        inc(v.memset(zbf[0:32, 0:32], 0.0), "e_dve")
        wait(sy, "e_dve")
        for side in range(2):
            for j in range(8):
                inc(sy.dma_start(out=d_nzp[side, :, j * 1408 + 1376:(j + 1) * 1408],
                                 in_=zbf[0:32, 0:32]), "s_sy", 16)

        wait(t, "e_pl")
        for cc in range(KC):
            c4 = cc % 4
            for tt in range(4):
                ins = t.matmul(pxs[tt][0:32, 0:512],
                               indm[:, 80 - 8 * c4: 80 - 8 * c4 + 32],
                               y1res[:, cc * T + tt * 512: cc * T + (tt + 1) * 512],
                               start=(cc == 0), stop=(cc == KC - 1))
            inc(ins, "e_pe")
        wait(v, "e_pe")
        for tt in range(4):
            inc(v.tensor_copy(Xs[:, tt * 512:(tt + 1) * 512],
                              pxs[tt][0:32, 0:512]), "e_dve")
        xs_v = V["e_dve"]

        wait(sy, "s_sy", V["s_sy"])
        deq_done = [0, 0]
        mm_gu = [0, 0]
        int_sp = [None, None]
        xsd_mm = 0
        pgu_free = [("e_sc", 0), ("e_dve", 0)]
        for b in range(NB):
            c11 = b % 11
            for side, (qq, ss) in enumerate(((gq, gsr), (uq, usr))):
                pn = "s_h0" if side == 0 else "s_h1"
                wait(sy, "e_dve", deq_done[side])
                inc(sy.dma_start(
                    out=qb[side][:].rearrange("p (c f) -> p c f", c=4),
                    in_=qq.rearrange("(c p) n -> p c n", p=128)[:, :, b * 128:(b + 1) * 128]),
                    pn, 16)
                inc(sy.dma_start(
                    out=sb_[side][:].rearrange("p (c f) -> p c f", c=4),
                    in_=ss.rearrange("(c p) n -> p c n", p=128)[:, :, b * 128:(b + 1) * 128]),
                    pn, 16)
                q_ld = (pn, V[pn])
                pnz = "s_m0" if side == 0 else "s_m1"
                wait(sy, "e_pe", mm_gu[side])
                inc(sy.dma_start(out=nzt[side][:],
                                 in_=d_nzp[side, :, b * 128:(b + 1) * 128]),
                    pnz, 16)
                nz_ld = (pnz, V[pnz])
                wait(v, *q_ld)
                wait(v, "e_pe", mm_gu[side])
                for j in range(8):
                    inc(v.tensor_scalar(uscr[:], qb[side][:], 4 * j, 15,
                                        ALU.logical_shift_right, ALU.bitwise_and),
                        "e_dve")
                    inc(v.tensor_tensor(wgu[side][:, j * 512:(j + 1) * 512],
                                        uscr[:], sb_[side][:], ALU.mult), "e_dve")
                deq_done[side] = V["e_dve"]
                if side == 0:
                    nz_g = nz_ld
                else:
                    nz_u = nz_ld
            for tt in range(4):
                for side in range(2):
                    wait(t, "e_dve", deq_done[side])
                    wait(t, *(nz_g if side == 0 else nz_u))
                    wait(t, "e_dve", xs_v)
                    sn, svv = pgu_free[side]
                    wait(t, sn, svv)
                    pp = pgu[side]
                    t.matmul(pp[:, 0:512], nzt[side][:],
                             Xs[:, tt * 512:(tt + 1) * 512],
                             start=True, stop=False)
                    for kkc in range(KC):
                        ins = t.matmul(
                            pp[:, 0:512], wgu[side][:, kkc * 128:(kkc + 1) * 128],
                            y1res[:, kkc * T + tt * 512: kkc * T + (tt + 1) * 512],
                            start=False, stop=(kkc == KC - 1))
                    mm_gu[side] = inc(ins, "e_pe")
                wait(s, "e_pe", mm_gu[1])
                inc(s.activation(sgate[:], pgu[0][:, 0:512], AF.Silu), "e_sc")
                sgv = V["e_sc"]
                pgu_free[0] = ("e_sc", sgv)
                ib = intb[tt % 2]
                wait(v, "e_sc", sgv)
                if int_sp[tt % 2] is not None:
                    wait(v, *int_sp[tt % 2])
                wait(v, "e_pe", xsd_mm)
                inc(v.tensor_tensor(ib[:], sgate[:], pgu[1][:, 0:512], ALU.mult),
                    "e_dve")
                ibv = V["e_dve"]
                pgu_free[1] = ("e_dve", ibv)
                wait(sy, "e_dve", ibv)
                pis = "s_c0" if tt % 2 == 0 else "s_c1"
                inc(sy.dma_start(out=d_int[b, :, tt * 512:(tt + 1) * 512],
                                 in_=ib[:]), pis, 16)
                int_sp[tt % 2] = (pis, V[pis])
                wait(t, "e_dve", ibv)
                ins = t.matmul(pxs[tt][0:86, 0:512],
                               indm[:, 80 - 8 * c11: 80 - 8 * c11 + 86],
                               ib[:], start=(b == 0), stop=(b == NB - 1))
                xsd_mm = inc(ins, "e_pe")
        wait(v, "e_pe", xsd_mm)
        for tt in range(4):
            inc(v.tensor_copy(Xsd[:, tt * 512:(tt + 1) * 512],
                              pxs[tt][0:86, 0:512]), "e_dve")
        zw2 = V["e_dve"]
        wait(sy, "e_dve", zw2)
        inc(sy.dma_start(out=zint[0:86, 0:512], in_=dz[:, :]), "s_sy", 16)
        for j in range(8):
            wait(sy, "e_dve", zw2)
            inc(sy.dma_start(out=zsc[0:86, 0:512],
                             in_=dso[:, j * 512:(j + 1) * 512]), "s_sy", 16)
            wait(v, "s_sy")
            inc(v.tensor_scalar(zf[0:86, 0:512].bitcast(I32), zint[0:86, 0:512],
                                4 * j, 15, ALU.logical_shift_right,
                                ALU.bitwise_and), "e_dve")
            inc(v.tensor_scalar(zf[0:86, 0:512], zf[0:86, 0:512].bitcast(I32),
                                -1.0, -1.0, ALU.mult, ALU.add), "e_dve")
            inc(v.tensor_tensor(nzd[0:86, j * 512:(j + 1) * 512], zf[0:86, 0:512],
                                zsc[0:86, 0:512], ALU.mult), "e_dve")
            zw2 = V["e_dve"]

    p5a()
    for nm in ["e_pe", "e_dve", "e_sc", "s_sy", "s_yn", "s_h0", "s_h1",
               "s_m0", "s_m1", "s_c0", "s_c1"]:
        wait(g_raw, nm, V[nm])
    bar = inc(g_raw.engine_nop(), "e_pl")
    for eng in (t, v_raw, s_raw, sy):
        eng.wait_ge(SEMS["e_pl"], bar)
    free_to(mk_base)

    def p5b():
        dqb = [sbt_s("dqb0", [128, D], I32), sbt_s("dqb1", [128, D], I32)]
        dsb = [sbt_s("dsb0", [128, D], F16), sbt_s("dsb1", [128, D], F16)]
        uscr2 = sbt_s("uscr2", [128, D], I32)
        wdb = [sbt_s("wdb0", [128, D], BF16), sbt_s("wdb1", [128, D], BF16)]
        dq_ld = [None, None]
        deq = [0, 0]
        wsp = [None, None]
        for c in range(11):
            bi = c % 2
            pn = "s_h0" if bi == 0 else "s_h1"
            wait(sy, "e_dve", deq[bi])
            inc(sy.dma_start(out=dqb[bi][:], in_=dq[c * 128:(c + 1) * 128, :]),
                pn, 16)
            inc(sy.dma_start(out=dsb[bi][:], in_=dsr[c * 128:(c + 1) * 128, :]),
                pn, 16)
            dq_ld[bi] = (pn, V[pn])
            wait(v, *dq_ld[bi])
            for j in range(8):
                inc(v.tensor_scalar(uscr2[:], dqb[bi][:], 4 * j, 15,
                                    ALU.logical_shift_right, ALU.bitwise_and),
                    "e_dve")
                wbi = j % 2
                if wsp[wbi] is not None:
                    wait(v, *wsp[wbi])
                inc(v.tensor_tensor(wdb[wbi][:], uscr2[:], dsb[bi][:], ALU.mult),
                    "e_dve")
                wv = V["e_dve"]
                wait(sy, "e_dve", wv)
                psp = "s_m0" if wbi == 0 else "s_m1"
                inc(sy.dma_start(out=d_wd[j * 11 + c, :, :], in_=wdb[wbi][:]),
                    psp, 16)
                wsp[wbi] = (psp, V[psp])
            deq[bi] = V["e_dve"]

    mk5 = mark_scope()
    p5b()
    for nm in ["e_dve", "s_m0", "s_m1", "s_h0", "s_h1"]:
        wait(g_raw, nm, V[nm])
    bar2 = inc(g_raw.engine_nop(), "e_pl")
    for eng in (t, v_raw, s_raw, sy):
        eng.wait_ge(SEMS["e_pl"], bar2)
    free_to(mk5)

    mu2 = sbt_s("mu2", [1, T], F32)
    m22 = sbt_s("m22", [1, T], F32)
    mk5c = mark_scope()

    def p5c():
        islab = sbt_s("islab", [128, NB * 512], BF16)
        wds = [sbt_s("wds0", [128, 512], BF16), sbt_s("wds1", [128, 512], BF16)]
        f16t = sbt_s("f16t", [128, 512], F16)
        y1rb = [sbt_s("y1rb0", [128, 512], F32), sbt_s("y1rb1", [128, 512], F32)]
        x2b = [sbt_s("x2b0", [128, 512], F32), sbt_s("x2b1", [128, 512], F32)]
        sqx2 = sbt_s("sqx2", [128, 512], BF16)
        oneb2 = sbt_s("oneb2", [128, 1], BF16)
        pdn = [pst_s(f"pdn{i}", [128, 512]) for i in range(4)]
        psm2 = pst_s("psm2", [128, 512])
        psq2 = pst_s("psq2", [128, 512])
        inc(g.memset(oneb2[:], 1.0), "e_pl")
        wait(t, "e_pl")
        w_ld = [None, None]
        wds_mm = [0, 0]
        ev2 = [0, 0, 0, 0]
        x2sp = [None, None]
        stat_mm = 0
        stat_ev = 0
        evb = [0, 0]
        for tt in range(4):
            wait(sy, "e_pe")
            for b in range(NB):
                inc(sy.dma_start(out=islab[:, b * 512:(b + 1) * 512],
                                 in_=d_int[b, :, tt * 512:(tt + 1) * 512]),
                    "s_cc", 16)
            slab_v = V["s_cc"]
            for mg in range(8):
                for mm4 in range(4):
                    m = mg * 4 + mm4
                    wait(t, "e_dve", ev2[mm4])
                    wait(t, "s_cc", slab_v)
                    ins = t.matmul(pdn[mm4][:, 0:512], nzd[:, m * 128:(m + 1) * 128],
                                   Xsd[:, tt * 512:(tt + 1) * 512],
                                   start=True, stop=False)
                for kb in range(NB):
                    kp = 96 if kb % 11 == 10 else 128
                    bi = kb % 2
                    pn = "s_h0" if bi == 0 else "s_h1"
                    wait(sy, "e_pe", wds_mm[bi])
                    inc(sy.dma_start(out=wds[bi][:],
                                     in_=d_wd[kb, :, mg * 512:(mg + 1) * 512]),
                        pn, 16)
                    w_ld[bi] = (pn, V[pn])
                    wait(t, *w_ld[bi])
                    for mm4 in range(4):
                        ins = t.matmul(pdn[mm4][:, 0:512],
                                       wds[bi][0:kp, mm4 * 128:(mm4 + 1) * 128],
                                       islab[0:kp, kb * 512:(kb + 1) * 512],
                                       start=False, stop=(kb == NB - 1))
                    wds_mm[bi] = inc(ins, "e_pe")
                mmv = V["e_pe"]
                for mm4 in range(4):
                    m = mg * 4 + mm4
                    bi = m % 2
                    pxn = "s_m0" if bi == 0 else "s_m1"
                    wait(sy, "e_dve", evb[bi])
                    inc(sy.dma_start(out=y1rb[bi][:],
                                     in_=d_y1[m, :, tt * 512:(tt + 1) * 512]),
                        pxn, 16)
                    yl = (pxn, V[pxn])
                    wait(v, "e_pe", mmv)
                    wait(v, *yl)
                    if x2sp[bi] is not None:
                        wait(v, *x2sp[bi])
                    wait(v, "e_pe", stat_mm)
                    inc(v.tensor_copy(f16t[:], pdn[mm4][:, 0:512]), "e_dve")
                    inc(v.tensor_add(x2b[bi][:], f16t[:], y1rb[bi][:]), "e_dve")
                    ev2[mm4] = V["e_dve"]
                    evb[bi] = V["e_dve"]
                    wait(sy, "e_dve", evb[bi])
                    psp = "s_g0" if bi == 0 else "s_g1"
                    inc(sy.dma_start(out=d_x2[m, :, tt * 512:(tt + 1) * 512],
                                     in_=x2b[bi][:]), psp, 16)
                    x2sp[bi] = (psp, V[psp])
                    wait(s, "e_dve", evb[bi])
                    wait(s, "e_pe", stat_mm)
                    inc(s.activation(sqx2[:], x2b[bi][:], AF.Square), "e_sc")
                    sqv = V["e_sc"]
                    wait(t, "e_sc", sqv)
                    if tt == 0 and mg == 0 and mm4 == 0:
                        pass
                    if mg == 0 and mm4 == 0:
                        wait(t, "e_sc2", stat_ev)
                    t.matmul(psm2[0:1, 0:512], onescol[:], x2b[bi][:],
                             start=(mg == 0 and mm4 == 0),
                             stop=(mg == 7 and mm4 == 3))
                    ins = t.matmul(psq2[0:1, 0:512], oneb2[:], sqx2[:],
                                   start=(mg == 0 and mm4 == 0),
                                   stop=(mg == 7 and mm4 == 3))
                    stat_mm = inc(ins, "e_pe")
            wait(s, "e_pe", stat_mm)
            inc(s.copy(mu2[0:1, tt * 512:(tt + 1) * 512], psm2[0:1, 0:512]),
                "e_sc2")
            inc(s.copy(m22[0:1, tt * 512:(tt + 1) * 512], psq2[0:1, 0:512]),
                "e_sc2")
            stat_ev = V["e_sc2"]

    p5c()
    for nm in ["e_pe", "e_dve", "e_sc", "e_sc2", "s_cc", "s_m0", "s_m1",
               "s_h0", "s_h1", "s_g0", "s_g1"]:
        wait(g_raw, nm, V[nm])
    bar3 = inc(g_raw.engine_nop(), "e_pl")
    for eng in (t, v_raw, s_raw, sy):
        eng.wait_ge(SEMS["e_pl"], bar3)
    free_to(mk5c)

    def p5d():
        MUR2 = sbt_s("MUR2", [128, T], F32)
        RSR2 = sbt_s("RSR2", [128, T], F32)
        vrow2 = sbt_s("vrow2", [1, T], F32)
        x2rb = [sbt_s("x2rb0", [128, T], F32), sbt_s("x2rb1", [128, T], F32)]
        orb = [sbt_s("orb0", [128, T], F32), sbt_s("orb1", [128, T], F32)]
        pr5 = [pst_s("pr50", [128, 512]), pst_s("pr51", [128, 512])]
        inc(v.tensor_single_scalar(mu2[0:1, :], mu2[0:1, :], 1.0 / D, ALU.mult),
            "e_dve")
        inc(v.tensor_single_scalar(m22[0:1, :], m22[0:1, :], 1.0 / D, ALU.mult),
            "e_dve")
        inc(v.tensor_tensor(vrow2[0:1, :], mu2[0:1, :], mu2[0:1, :], ALU.mult),
            "e_dve")
        inc(v.tensor_sub(vrow2[0:1, :], m22[0:1, :], vrow2[0:1, :]), "e_dve")
        vv = V["e_dve"]
        wait(s, "e_dve", vv)
        inc(s.activation(vrow2[0:1, :], vrow2[0:1, :], AF.Ln,
                         bias=eps_col[0:1, 0:1]), "e_sc")
        inc(s.activation(vrow2[0:1, :], vrow2[0:1, :], AF.Exp, scale=-0.5),
            "e_sc")
        rsv = V["e_sc"]
        wait(t, "e_sc", rsv)
        pb_ev = [0, 0]
        for ri in range(8):
            src, dst = (mu2, MUR2) if ri < 4 else (vrow2, RSR2)
            off = (ri % 4) * 512
            pb = pr5[ri % 2]
            wait(t, "e_sc2", pb_ev[ri % 2])
            inc(t.matmul(pb[:, 0:512], onesrow_f[:], src[0:1, off:off + 512],
                         start=True, stop=True), "e_pe")
            wait(s, "e_pe")
            inc(s.copy(dst[:, off:off + 512], pb[:, 0:512]), "e_sc2")
            pb_ev[ri % 2] = V["e_sc2"]
        wait(v, "e_sc2")
        ap = [0, 0]
        osp = [None, None]
        for m in range(KC):
            bi = m % 2
            pxn = "s_m0" if bi == 0 else "s_m1"
            wait(sy, "e_dve", ap[bi])
            inc(sy.dma_start(out=x2rb[bi][:], in_=d_x2[m, :, :]), pxn, 16)
            xl = (pxn, V[pxn])
            wait(v, *xl)
            if osp[bi] is not None:
                wait(v, *osp[bi])
            inc(v.tensor_sub(x2rb[bi][:], x2rb[bi][:], MUR2[:]), "e_dve")
            inc(v.tensor_tensor(orb[bi][:], x2rb[bi][:], RSR2[:], ALU.mult),
                "e_dve")
            ap[bi] = V["e_dve"]
            wait(sy, "e_dve", ap[bi])
            pso = "s_g0" if bi == 0 else "s_g1"
            inc(sy.dma_start(out=out[m, :, :], in_=orb[bi][:]), pso, 16)
            osp[bi] = (pso, V[pso])

    p5d()
    for nm in ["s_gp", "s_gpa", "s_gpb", "s_sy", "s_sc", "s_sca", "s_scb",
               "s_h0", "s_h1", "s_c0", "s_c1", "s_m0", "s_m1", "s_g0", "s_g1",
               "s_cc", "s_yn", "s_sy2"]:
        wait(sy, nm)
    free_scope()


def _dt_path(nc, V, SEMS, inc, wait, g, v, s, t, sy, stg, stg1, cscr, hbcb, dtT,
             tc_sb, etc_sb, acol, dtb_sb, identf, ptf, zeros128, d_chl, d_ncat,
             spls, onescol):
    if True:
        t0 = stg1[:, 0:TA]
        hl = hbcb[:, 0:TA]
        wait(v, "e_dve")
        vstg = inc(v.tensor_scalar(t0, stg[:, PADC:W_ALL], dtb_sb[:, 0:1], None,
                                   ALU.add), "e_dve")
        wait(s, "e_dve", vstg)
        vabs = inc(s.activation(cscr[:], t0, AF.Abs), "e_sc")
        inc(s.activation(cscr[:], cscr[:], AF.Exp, scale=-1.0), "e_sc")
        inc(s.activation(cscr[:], cscr[:], AF.Ln, bias=onescol[:, 0:1]), "e_sc")
        wait(v, "e_sc", vabs)
        inc(v.tensor_single_scalar(t0, t0, 0.0, ALU.max), "e_dve")
        wait(v, "e_sc")
        inc(v.tensor_add(t0, t0, cscr[:]), "e_dve")
        tev = []
        for c in range(NC_):
            wait(t, "e_dve")
            if c >= 1:
                wait(t, "e_dve2", tev[c - 1])
            inc(t.transpose(ptf[:], t0[:, c * 128:(c + 1) * 128], identf[:]), "e_pe2")
            wait(v, "e_pe2")
            tev.append(inc(v.tensor_copy(dtT[:, c * 128:(c + 1) * 128], ptf[:]),
                           "e_dve2"))
        inc(v.tensor_scalar(cscr[:], t0, acol[:, 0:1], None, ALU.mult), "e_dve")
        wait(v, "e_dve")
        for c in range(NC_):
            inc(v.tensor_tensor_scan(t0[:, c * 128:(c + 1) * 128],
                                     cscr[:, c * 128:(c + 1) * 128],
                                     zeros128[:], 0.0, ALU.add, ALU.add), "e_dve")
        wait(v, "e_dve")
        inc(v.tensor_copy(tc_sb[:], t0[:, 127::128]), "e_dve")
        wait(s, "e_dve")
        inc(s.activation(etc_sb[:], tc_sb[:], AF.Exp), "e_sc")
        inc(v.tensor_copy(hl, t0), "e_dve")
        wait(v, "e_dve")
        inc(v.tensor_sub(cscr[:], t0, hl), "e_dve")
        wait(sy, "e_dve")
        inc(sy.dma_start(out=d_chl[0, :, :], in_=hl), "s_sy", 16)
        wait(v, "s_sy")
        inc(v.tensor_copy(hl, cscr[:]), "e_dve")
        wait(sy, "e_dve")
        inc(sy.dma_start(out=d_chl[1, :, :], in_=hl), "s_sy", 16)
        inc(v.tensor_single_scalar(t0, t0, -1.0, ALU.mult), "e_dve")
        spf = []
        for c in range(NC_):
            wait(t, "e_dve")
            if c >= 1:
                wait(t, "e_dve2", V["e_dve2"])
            inc(t.transpose(ptf[:], t0[:, c * 128:(c + 1) * 128], identf[:]), "e_pe2")
            wait(v, "e_pe2")
            if c >= 2:
                wait(v, *spf[c - 2])
            sp = spls[c % 2]
            inc(v.tensor_copy(cscr[:, (c % 2) * 128:(c % 2) * 128 + 128], ptf[:]),
                "e_dve2")
            wait(sy, "e_dve2")
            psn = "s_sca" if c % 2 == 0 else "s_scb"
            spf.append((psn, inc(sy.dma_start(
                out=d_ncat[:, c * 128:(c + 1) * 128],
                in_=cscr[:, (c % 2) * 128:(c % 2) * 128 + 128]), psn, 16)))
        return dict(stg=("e_dve", vstg),
                    scr=[("e_dve", V["e_dve"]), ("s_sca", V["s_sca"]),
                         ("s_scb", V["s_scb"]), ("e_dve2", V["e_dve2"]),
                         ("s_sy", V["s_sy"])])


def _prep_inputs(inputs):
    sd = sigma_d_perm()
    x = inputs["L0_attn_x"]
    x2d = x.reshape(16384, 4096)
    xfm = np.ascontiguousarray(x2d.T[sd])

    W1 = inputs["L0_attn_in_proj_w"][:, sd]
    W1t = np.ascontiguousarray(
        W1.reshape(145, 128, 32, 128).transpose(0, 2, 3, 1)).reshape(145, -1)
    W1t = np.concatenate([W1t, np.zeros((1, W1t.shape[1]), np.float32)], 0)

    W2 = inputs["L0_attn_out_proj_w"][sd, :]
    W2t = np.ascontiguousarray(
        W2.reshape(32, 128, 64, 128).transpose(0, 2, 3, 1)).reshape(32, -1)

    jcols = [np.arange(j, INTER, 8) for j in range(8)]

    def pad_cols_i(a):
        o = np.zeros((a.shape[0], INTP), a.dtype)
        for j in range(8):
            o[:, j * 1408: j * 1408 + 1376] = a[:, jcols[j]]
        return o

    shared = dict(
        w1=W1t, w2=W2t,
        normw=np.ascontiguousarray(inputs["L0_attn_norm_w"][sd].reshape(32, 128).T),
        convw=np.ascontiguousarray(inputs["L0_attn_conv_w"].reshape(80, 128, 4)),
        convb=np.ascontiguousarray(inputs["L0_attn_conv_b"].reshape(80, 128).T),
        dtbias=np.ascontiguousarray(inputs["L0_attn_dt_bias"].reshape(128, 1)),
        alog=np.ascontiguousarray(inputs["L0_attn_A_log"].reshape(128, 1)),
        dprep=np.ascontiguousarray(np.repeat(inputs["L0_attn_Dp"], 64).reshape(64, 128).T),
        gnwi=np.ascontiguousarray(inputs["L0_attn_gnorm_w"].reshape(64, 128).T),
        gq=pad_cols_i(inputs["L0_ffn_gate_qweight"]),
        gsr=pad_cols_i(np.repeat(inputs["L0_ffn_gate_scales"], 16, axis=0)),
        gz=inputs["L0_ffn_gate_zeros"],
        gso=pad_cols_i(inputs["L0_ffn_gate_scales"]),
        uq=pad_cols_i(inputs["L0_ffn_up_qweight"]),
        usr=pad_cols_i(np.repeat(inputs["L0_ffn_up_scales"], 16, axis=0)),
        uz=inputs["L0_ffn_up_zeros"],
        uso=pad_cols_i(inputs["L0_ffn_up_scales"]),
        dq=np.concatenate([inputs["L0_ffn_down_qweight"][:, sd],
                           np.zeros((32, 4096), np.int32)], 0),
        dsr=np.concatenate([np.repeat(inputs["L0_ffn_down_scales"], 16, axis=0)[:, sd],
                            np.zeros((32, 4096), np.float16)], 0),
        dz=inputs["L0_ffn_down_zeros"],
        dso=np.ascontiguousarray(inputs["L0_ffn_down_scales"][:, sd]),
    )

    in_maps = []
    for k in range(8):
        xc = np.zeros((4096, W_ALL), np.float32)
        t0 = 2048 * k
        xc[:, HW:] = xfm[:, t0:t0 + T]
        fl = np.zeros((128, 1), np.float32)
        if k not in (0, 4):
            xc[:, 1:HW] = xfm[:, t0 - TH - 3: t0]
            fl[:] = 1.0
        m = dict(shared)
        m["xin"] = np.ascontiguousarray(xc.reshape(32, 128, W_ALL))
        m["flag"] = fl
        in_maps.append(m)
    return in_maps


def kernel(**inputs):
    in_maps = _prep_inputs(inputs)
    nc = build_program()
    res = run_bass_kernel_spmd(nc, in_maps, core_ids=list(range(8)))
    sd = sigma_d_perm()
    outs = [np.asarray(res.results[k]["out"]).reshape(4096, T) for k in range(8)]
    ofm = np.concatenate(outs, axis=1)
    out2d = np.empty((16384, 4096), np.float32)
    out2d[:, sd] = ofm.T
    return out2d.reshape(2, 8192, 4096)


if __name__ == "__main__":
    import time
    t0_ = time.time()
    nc = build_program(debug=False)
    n = sum(1 for _ in nc.all_instructions())
    print(f"built {n} instructions in {time.time()-t0_:.1f}s")


# version 03 (final)
# speedup vs baseline: 1.1023x; 1.1023x over previous
import numpy as np
import ml_dtypes
from contextlib import ExitStack

import concourse.bass as bass
import concourse.mybir as mybir
from concourse.bass_utils import run_bass_kernel_spmd

F32 = mybir.dt.float32
F16 = mybir.dt.float16
BF16 = mybir.dt.bfloat16
I32 = mybir.dt.int32
AF = mybir.ActivationFunctionType
ALU = mybir.AluOpType

D = 4096
KC = 32
PROJ = 18560
MPROJ = 145
DI = 8192
H = 128
P = 64
G = 8
N = 128
EPS = 1e-5

T = 2048
TH = 256
NCR = 16
NCH = 2
NC_ = NCR + NCH
TA = TH + T
PADC = 4
HW = PADC + TH
W_ALL = PADC + TA

INTER = 11008
INTP = 11264
NB = 88

ORD = [144] + list(range(128, 144)) + list(range(64, 128)) + list(range(64))
N_HBC = 81

DEBUG_N = 0


def sigma_d_perm():
    return np.arange(4096).reshape(512, 8).T.reshape(-1).copy()


def build_program(debug=False, nphase=9, p2n=None, no_bt=False, no_spill=False, p3n=64):
    nc = bass.Bass("TRN2", target_bir_lowering=False)
    ap = lambda t: t.ap()

    xin = ap(nc.dram_tensor("xin", [KC, 128, W_ALL], F32, kind="ExternalInput"))
    w1 = ap(nc.dram_tensor("w1", [MPROJ + 1, KC * 128 * 128], F32, kind="ExternalInput"))
    w2 = ap(nc.dram_tensor("w2", [KC, 64 * 128 * 128], F32, kind="ExternalInput"))
    normw = ap(nc.dram_tensor("normw", [128, KC], F32, kind="ExternalInput"))
    convw = ap(nc.dram_tensor("convw", [80, 128, 4], F32, kind="ExternalInput"))
    convb = ap(nc.dram_tensor("convb", [128, 80], F32, kind="ExternalInput"))
    dtbias = ap(nc.dram_tensor("dtbias", [128, 1], F32, kind="ExternalInput"))
    alog = ap(nc.dram_tensor("alog", [128, 1], F32, kind="ExternalInput"))
    dprep = ap(nc.dram_tensor("dprep", [128, 64], F32, kind="ExternalInput"))
    gnwi = ap(nc.dram_tensor("gnwi", [128, 64], F32, kind="ExternalInput"))
    flag = ap(nc.dram_tensor("flag", [128, 1], F32, kind="ExternalInput"))
    gq = ap(nc.dram_tensor("gq", [512, INTP], I32, kind="ExternalInput"))
    gsr = ap(nc.dram_tensor("gsr", [512, INTP], F16, kind="ExternalInput"))
    gz = ap(nc.dram_tensor("gz", [32, 1376], I32, kind="ExternalInput"))
    gso = ap(nc.dram_tensor("gso", [32, INTP], F16, kind="ExternalInput"))
    uq = ap(nc.dram_tensor("uq", [512, INTP], I32, kind="ExternalInput"))
    usr = ap(nc.dram_tensor("usr", [512, INTP], F16, kind="ExternalInput"))
    uz = ap(nc.dram_tensor("uz", [32, 1376], I32, kind="ExternalInput"))
    uso = ap(nc.dram_tensor("uso", [32, INTP], F16, kind="ExternalInput"))
    dq = ap(nc.dram_tensor("dq", [1408, D], I32, kind="ExternalInput"))
    dsr = ap(nc.dram_tensor("dsr", [1408, D], F16, kind="ExternalInput"))
    dz = ap(nc.dram_tensor("dz", [86, 512], I32, kind="ExternalInput"))
    dso = ap(nc.dram_tensor("dso", [86, D], F16, kind="ExternalInput"))

    out = ap(nc.dram_tensor("out", [KC, 128, T], F32, kind="ExternalOutput"))
    dbg = ap(nc.dram_tensor("dbg", [16, 128, 2304], F32, kind="ExternalOutput")) \
        if debug else None

    d_hidt = ap(nc.dram_tensor("d_hidt", [64 * NC_, 128, 128], BF16, kind="Internal"))
    d_hidc = ap(nc.dram_tensor("d_hidc", [64 * NC_, 128, 128], BF16, kind="Internal"))
    d_b = ap(nc.dram_tensor("d_b", [G, NC_, 128, 128], BF16, kind="Internal"))
    d_bt = ap(nc.dram_tensor("d_bt", [G, NC_, 128, 128], BF16, kind="Internal"))
    d_c = ap(nc.dram_tensor("d_c", [G, NC_, 128, 128], BF16, kind="Internal"))
    d_chl = ap(nc.dram_tensor("d_chl", [2, 128, TA], BF16, kind="Internal"))
    d_ncat = ap(nc.dram_tensor("d_ncat", [128, NC_ * 128], F32, kind="Internal"))
    d_sg = ap(nc.dram_tensor("d_sg", [64, NCR, 128, 128], F32, kind="Internal"))
    d_yn = ap(nc.dram_tensor("d_yn", [64, 128, T], BF16, kind="Internal"))
    d_x1 = ap(nc.dram_tensor("d_x1", [KC, 128, T], F32, kind="Internal"))
    d_y1 = ap(nc.dram_tensor("d_y1", [KC, 128, T], F32, kind="Internal"))
    d_int = ap(nc.dram_tensor("d_int", [NB, 128, T], BF16, kind="Internal"))
    d_wd = ap(nc.dram_tensor("d_wd", [NB, 128, D], BF16, kind="Internal"))
    d_x2 = ap(nc.dram_tensor("d_x2", [KC, 128, T], F32, kind="Internal"))
    d_bnc = ap(nc.dram_tensor("d_bnc", [8, 4096], F32, kind="Internal"))

    with ExitStack() as ctx:
        def sbt(name, shape, dt):
            return ctx.enter_context(nc.sbuf_tensor(name, shape, dt))

        def pst(name, shape, dt=F32):
            return ctx.enter_context(nc.psum_tensor(name, shape, dt))

        _scoped = []

        def sbt_s(name, shape, dt):
            cm = nc.sbuf_tensor(name, shape, dt)
            _scoped.append(cm)
            return cm.__enter__()

        def pst_s(name, shape, dt=F32):
            cm = nc.psum_tensor(name, shape, dt)
            _scoped.append(cm)
            return cm.__enter__()

        def free_scope():
            for cm in reversed(_scoped):
                cm.__exit__(None, None, None)
            _scoped.clear()

        def mark_scope():
            return len(_scoped)

        def free_to(mk):
            while len(_scoped) > mk:
                _scoped.pop().__exit__(None, None, None)

        SEMS = {}
        V = {}

        def sem(name):
            SEMS[name] = ctx.enter_context(nc.semaphore(name))
            V[name] = 0

        for nm in ["s_gp", "s_sy", "s_sc", "s_sy2", "s_gpa", "s_gpb",
                   "s_sca", "s_scb", "s_h0", "s_h1", "s_c0", "s_c1",
                   "s_m0", "s_m1", "s_g0", "s_g1", "s_cc", "s_yn",
                   "e_pe", "e_dve", "e_sc", "e_pl",
                   "e_pe2", "e_dve2", "e_sc2", "e_pl2"]:
            sem(nm)

        def inc(ins, name, k=1):
            ins.then_inc(SEMS[name], k)
            V[name] += k
            return V[name]

        def wait(eng, name, v=None):
            eng.wait_ge(SEMS[name], V[name] if v is None else v)

        g_raw = nc.gpsimd
        v_raw = nc.vector
        s_raw = nc.scalar
        t = nc.tensor
        sy = nc.sync

        _SELF = {}

        class _SS:
            _PASS = {"wait_ge", "dma_start", "sem_inc", "If", "Else", "snap"}

            def __init__(self, eng, sems):
                self._e = eng
                self._s = sems

            def __getattr__(self, name):
                real = getattr(self._e, name)
                if name in self._PASS or not callable(real):
                    return real

                def wrapped(*a, **k):
                    for sn in self._s:
                        if V[sn] > 0:
                            self._e.wait_ge(SEMS[sn], V[sn])
                    return real(*a, **k)
                return wrapped

        g = _SS(g_raw, ["e_pl", "e_pl2"])
        v = _SS(v_raw, ["e_dve", "e_dve2"])
        s = _SS(s_raw, ["e_sc", "e_sc2"])
        ESEM = {id(v): "e_dve", id(g): "e_pl", id(s): "e_sc"}

        identb = sbt("identb", [128, 128], BF16)
        identf = sbt("identf", [128, 128], F32)
        onescol = sbt("onescol", [128, 1], F32)
        onesrow_f = sbt("onesrow_f", [1, 128], F32)
        onesrow_b = sbt("onesrow_b", [1, 128], BF16)
        zeros128 = sbt("zeros128", [128, 128], F32)
        normw_sb = sbt("normw_sb", [128, KC], F32)
        convw_sb = sbt("convw_sb", [128, 320], F32)
        convb_sb = sbt("convb_sb", [128, 80], F32)
        dtb_sb = sbt("dtb_sb", [128, 1], F32)
        acol = sbt("acol", [128, 1], F32)
        dp_sb = sbt("dp_sb", [128, 64], F32)
        gnw_sb = sbt("gnw_sb", [128, 64], F32)
        flag_sb = sbt("flag_sb", [128, 1], F32)
        indm = sbt("indm", [128, 166], BF16)
        tc_sb = sbt("tc_sb", [128, NC_], F32)
        etc_sb = sbt("etc_sb", [128, NC_], F32)
        eps_col = sbt("eps_col", [128, 1], F32)

        for dst, src in [(normw_sb, normw), (convb_sb, convb), (dtb_sb, dtbias),
                         (acol, alog), (dp_sb, dprep), (gnw_sb, gnwi),
                         (flag_sb, flag)]:
            inc(g.dma_start(out=dst[:], in_=src[:]), "s_gp", 16)
        inc(g.dma_start(out=convw_sb[:].rearrange("p (c k) -> p c k", k=4),
                        in_=convw.rearrange("c p k -> p c k")), "s_gp", 16)
        for tile, val in [(identb, 1.0), (identf, 1.0), (onescol, 1.0),
                          (onesrow_f, 1.0), (onesrow_b, 1.0), (zeros128, 0.0),
                          (eps_col, EPS)]:
            inc(g.memset(tile[:], val), "e_pl")
        wait(g, "e_pl")
        inc(g.affine_select(identb[:], identb[:], [[1, 128]], ALU.is_equal, 0.0,
                            base=0, channel_multiplier=-1), "e_pl")
        inc(g.affine_select(identf[:], identf[:], [[1, 128]], ALU.is_equal, 0.0,
                            base=0, channel_multiplier=-1), "e_pl")
        with nc.sbuf_tensor("iota_r", [128, 166], F32) as iota_r, \
             nc.sbuf_tensor("iota_c", [128, 1], I32) as iota_c, \
             nc.sbuf_tensor("qcol", [128, 1], I32) as qcol, \
             nc.sbuf_tensor("qcolf", [128, 1], F32) as qcolf, \
             nc.sbuf_tensor("indf", [128, 166], F32) as indf:
            inc(g.iota(iota_r[:], [[1, 166]], base=0, channel_multiplier=0,
                       allow_small_or_imprecise_dtypes=True), "e_pl")
            inc(g.iota(iota_c[:], [[0, 1]], base=0, channel_multiplier=1), "e_pl")
            wait(v, "e_pl")
            inc(v.tensor_single_scalar(qcol[:], iota_c[:], 4, ALU.logical_shift_right), "e_dve")
            wait(v, "e_dve")
            inc(v.tensor_copy(qcolf[:], qcol[:]), "e_dve")
            wait(v, "e_dve")
            inc(v.tensor_scalar(indf[:], iota_r[:], qcolf[:, 0:1], 80.0,
                                ALU.subtract, ALU.is_equal), "e_dve")
            wait(v, "e_dve")
            inc(v.tensor_copy(indm[:], indf[:]), "e_dve")
        wait(s, "s_gp")
        inc(s.activation(acol[:], acol[:], AF.Exp), "e_sc")
        wait(v, "e_sc")
        inc(v.tensor_single_scalar(acol[:], acol[:], -1.0, ALU.mult), "e_dve")

        h_real = sbt_s("h_real", [128, KC * T], BF16)
        h_halo = sbt_s("h_halo", [128, KC * HW], BF16)

        with nc.sbuf_tensor("xb0", [128, W_ALL], F32) as xb0, \
             nc.sbuf_tensor("xb1", [128, W_ALL], F32) as xb1, \
             nc.sbuf_tensor("sq0", [128, W_ALL], BF16) as sq0, \
             nc.sbuf_tensor("sq1", [128, W_ALL], BF16) as sq1, \
             nc.sbuf_tensor("onescol_b", [128, 1], BF16) as onescol_b, \
             nc.sbuf_tensor("rrow", [1, W_ALL], F32) as rrow, \
             nc.sbuf_tensor("rsrow", [128, W_ALL], F32) as rsrow, \
             nc.psum_tensor("pstat0", [1, 512], F32) as pstat0, \
             nc.psum_tensor("pstat1", [1, 512], F32) as pstat1, \
             nc.psum_tensor("pstat2", [1, 512], F32) as pstat2, \
             nc.psum_tensor("pstat3", [1, 512], F32) as pstat3, \
             nc.psum_tensor("pstat4", [1, 512], F32) as pstat4, \
             nc.psum_tensor("pbc0", [128, 512], F32) as pbc0, \
             nc.psum_tensor("pbc1", [128, 512], F32) as pbc1:
            inc(g.memset(onescol_b[:], 1.0), "e_pl")
            xb = [xb0, xb1]
            sq = [sq0, sq1]
            x_ld = []
            sq_done = []
            mm_done = []
            REGS = [(0, 512), (512, 512), (1024, 512), (1536, 512), (2048, W_ALL - 1 - 2048)]
            wait(t, "s_gp")
            wait(t, "e_pl")
            for kk in range(KC):
                bi = kk % 2
                if kk >= 2:
                    wait(g, "e_sc", sq_done[kk - 2])
                pn = "s_gpa" if bi == 0 else "s_gpb"
                x_ld.append((pn, inc(g.dma_start(out=xb[bi][:], in_=xin[kk, :, :]), pn, 16)))
                wait(s, *x_ld[kk])
                if kk >= 2:
                    wait(s, "e_pe", mm_done[kk - 2])
                sq_done.append(inc(s.activation(sq[bi][:, 1:W_ALL], xb[bi][:, 1:W_ALL],
                                                AF.Square), "e_sc"))
                wait(t, "e_sc", sq_done[kk])
                pstats = [pstat0, pstat1, pstat2, pstat3, pstat4]
                for ri, (off, wd) in enumerate(REGS):
                    ins = t.matmul(pstats[ri][0:1, 0:wd], onescol_b[:],
                                   sq[bi][:, 1 + off: 1 + off + wd],
                                   start=(kk == 0), stop=(kk == KC - 1))
                mm_done.append(inc(ins, "e_pe"))
            wait(s, "e_pe")
            pstats = [pstat0, pstat1, pstat2, pstat3, pstat4]
            for ri, (off, wd) in enumerate(REGS):
                inc(s.copy(rrow[0:1, 1 + off: 1 + off + wd], pstats[ri][0:1, 0:wd]),
                    "e_sc")
            if debug:
                wait(g, "e_sc")
                inc(g.dma_start(out=dbg[11, 0:1, 0:2304], in_=rrow[0:1, 1:2305]),
                    "s_gp", 16)
                wait(s, "s_gp")
            inc(s.activation(rrow[0:1, 1:W_ALL], rrow[0:1, 1:W_ALL], AF.Ln,
                             bias=eps_col[0:1, 0:1], scale=1.0 / D), "e_sc")
            inc(s.activation(rrow[0:1, 1:W_ALL], rrow[0:1, 1:W_ALL], AF.Exp, scale=-0.5),
                "e_sc")
            pbc = [pbc0, pbc1]
            evd = []
            wait(t, "e_sc")
            for i, (off, wd) in enumerate(REGS):
                pb = pbc[i % 2]
                if i >= 2:
                    wait(t, "e_sc2", evd[i - 2])
                inc(t.matmul(pb[:, 0:wd], onesrow_f[:], rrow[0:1, 1 + off: 1 + off + wd],
                             start=True, stop=True), "e_pe")
                wait(s, "e_pe")
                evd.append(inc(s.copy(rsrow[:, 1 + off: 1 + off + wd], pb[:, 0:wd]),
                               "e_sc2"))
            inc(g.memset(rsrow[:, 0:1], 0.0), "e_pl")
            wait(v, "e_sc2")
            wait(v, "e_pl")
            wait(g, "e_sc", sq_done[KC - 1])
            app_done = []
            for kk in range(KC):
                bi = kk % 2
                if kk >= 2:
                    wait(g, "e_dve2", app_done[kk - 2])
                pn = "s_gpa" if bi == 0 else "s_gpb"
                x_ld.append((pn, inc(g.dma_start(out=xb[bi][:], in_=xin[kk, :, :]), pn, 16)))
                wait(v, *x_ld[-1])
                inc(v.scalar_tensor_tensor(
                    h_halo[:, kk * HW:(kk + 1) * HW], xb[bi][:, 0:HW],
                    normw_sb[:, kk:kk + 1], rsrow[:, 0:HW], ALU.mult, ALU.mult), "e_dve2")
                app_done.append(inc(v.scalar_tensor_tensor(
                    h_real[:, kk * T:(kk + 1) * T], xb[bi][:, HW:W_ALL],
                    normw_sb[:, kk:kk + 1], rsrow[:, HW:W_ALL], ALU.mult, ALU.mult),
                    "e_dve2"))
            v_h_done = V["e_dve2"]

        if nphase < 2:
            zo = sbt("zo", [128, T], F32)
            inc(g.memset(zo[:], 0.0), "s_gp")
            wait(sy, "s_gp")
            for kk in range(KC):
                inc(sy.dma_start(out=out[kk, :, :], in_=zo[:]), "s_sy2", 16)
            if debug:
                wait(g, "e_dve2", v_h_done)
                for i in range(4):
                    inc(g.dma_start(out=dbg[i, :, 0:T],
                                    in_=h_real[:, i * T: i * T + T]), "s_gp", 16)
                wait(g, "s_gp")
            wait(sy, "s_sy2")
            return nc
        wts = [sbt_s("wt0", [128, KC * 128], BF16), sbt_s("wt1", [128, KC * 128], BF16)]
        stgs = [sbt_s("stg0", [128, W_ALL], F32), sbt_s("stg1", [128, W_ALL], F32)]
        cscr = sbt_s("cscr", [128, TA], F32)
        hbcb = sbt_s("hbcb", [128, TA], BF16)
        dtT = sbt_s("dtT", [128, NC_ * 128], F32)
        spls = [sbt_s(f"spl{i}", [128, 128], BF16) for i in range(4)]

        phalo = pst_s("phalo", [128, 260])
        p4_ = phalo[:, 0:PADC]
        p256 = phalo[:, PADC:260]
        pbt = [pst_s(f"pb{i}", [128, 512]) for i in range(4)]
        ptp = [pst_s("ptp0", [128, 128], BF16), pst_s("ptp1", [128, 128], BF16)]
        ptf = pst_s("ptf", [128, 128], F32)

        w_ld = []
        mmv = []
        evv = []
        spl_last = {0: None, 1: None, 2: None, 3: None}
        stg_free = []
        scr_free = [("e_pe", 0)]

        def w1_load(mi):
            pn = "s_gpa" if mi % 2 == 0 else "s_gpb"
            w_ld.append((pn, inc(g.dma_start(
                out=wts[mi % 2][:].rearrange("p (k f) -> p k f", k=KC),
                in_=w1[ORD[mi] if mi < len(ORD) else MPROJ, :]
                .rearrange("(k p f) -> p k f", p=128, k=KC)), pn, 16)))

        wait(g_raw, "e_dve2", v_h_done)
        w1_load(0)
        w1_load(1)

        for mi, ordm in enumerate(ORD if p2n is None else ORD[:p2n]):
            is_hbc = mi < N_HBC
            kind = ("dt" if ordm == 144 else
                    "B" if 128 <= ordm < 136 else
                    "C" if 136 <= ordm < 144 else
                    "hid" if 64 <= ordm < 128 else "gate")
            wait(t, *w_ld[mi])
            if mi == 0:
                wait(t, "e_dve2", v_h_done)
            if mi >= 1:
                wait(t, "e_dve", evv[mi - 1])
            wt = wts[mi % 2]
            groups = ([(p4_, h_halo, 0, PADC, HW), (p256, h_halo, PADC, 256, HW)]
                      if is_hbc else [])
            groups += [(pbt[tt], h_real, tt * 512, 512, T) for tt in range(4)]
            for pbuf, hsrc, off, wd, blk in groups:
                pb_ap = pbuf[:, 0:wd] if not isinstance(pbuf, bass.AP) else pbuf
                for k in range(KC):
                    ins = t.matmul(pb_ap, wt[:, k * 128:(k + 1) * 128],
                                   hsrc[:, k * blk + off: k * blk + off + wd],
                                   start=(k == 0), stop=(k == KC - 1))
                inc(ins, "e_pe")
            mmv.append(V["e_pe"])
            if mi + 2 < len(ORD):
                wait(g, "e_pe", mmv[mi])
                w1_load(mi + 2)
            if mi == 1:
                for sn, sv in stg1_guard:
                    wait(v, sn, sv)
            if mi >= 2:
                sn, sv = stg_free[mi - 2]
                wait(v, sn, sv)
            wait(v, "e_pe", mmv[mi])
            stg = stgs[mi % 2]
            if is_hbc:
                inc(v.tensor_copy(stg[:, 0:PADC], p4_), "e_dve")
                inc(v.tensor_copy(stg[:, PADC:HW], p256), "e_dve")
            for tt in range(4):
                inc(v.tensor_copy(stg[:, HW + tt * 512: HW + (tt + 1) * 512],
                                  pbt[tt][:]), "e_dve")
            evv.append(V["e_dve"])

            if kind == "dt":
                fr = _dt_path(nc, V, SEMS, inc, wait, g, v, s, t, sy,
                              stg, stgs[1], cscr, hbcb, dtT, tc_sb, etc_sb,
                              acol, dtb_sb, identf, ptf, zeros128, d_chl,
                              d_ncat, spls, onescol)
                stg_free.append(fr["stg"])
                scr_free = fr["scr"]
                stg1_guard = fr["scr"]
            elif kind in ("B", "C", "hid"):
                ce = v
                cs = ESEM[id(ce)]
                for sn, sv in scr_free:
                    wait(ce, sn, sv)
                wait(ce, "e_dve", evv[mi])
                cc = ordm - 64
                w0 = convw_sb[:, cc * 4: cc * 4 + 1]
                w1c = convw_sb[:, cc * 4 + 1: cc * 4 + 2]
                w2c = convw_sb[:, cc * 4 + 2: cc * 4 + 3]
                w3c = convw_sb[:, cc * 4 + 3: cc * 4 + 4]
                bcol = convb_sb[:, cc:cc + 1]
                inc(ce.tensor_scalar(cscr[:], stg[:, 1:1 + TA], w0, bcol,
                                     ALU.mult, ALU.add), cs)
                inc(ce.scalar_tensor_tensor(cscr[:], stg[:, 2:2 + TA], w1c, cscr[:],
                                            ALU.mult, ALU.add), cs)
                inc(ce.scalar_tensor_tensor(cscr[:], stg[:, 3:3 + TA], w2c, cscr[:],
                                            ALU.mult, ALU.add), cs)
                tap4 = inc(ce.scalar_tensor_tensor(cscr[:], stg[:, 4:4 + TA], w3c,
                                                   cscr[:], ALU.mult, ALU.add), cs)
                stg_free.append((cs, tap4))
                wait(s, cs, tap4)
                silu_v = inc(s.activation(hbcb[:], cscr[:], AF.Silu), "e_sc")
                scr_free = [("e_sc", silu_v)]
                wait(sy, "e_sc", silu_v)
                if not no_spill:
                    if kind == "hid":
                        hh = ordm - 64
                        for c in range(NC_):
                            inc(sy.dma_start(out=d_hidc[hh * NC_ + c, :, :],
                                             in_=hbcb[:, c * 128:(c + 1) * 128]), "s_sy", 16)
                    else:
                        gi = ordm - 128 if kind == "B" else ordm - 136
                        dst = d_b if kind == "B" else d_c
                        for c in range(NC_):
                            inc(sy.dma_start(out=dst[gi, c, :, :],
                                             in_=hbcb[:, c * 128:(c + 1) * 128]), "s_sy", 16)
                scr_free.append(("s_sy", V["s_sy"]))
                if kind in ("hid", "B") and not no_bt:
                    tev = []
                    spf = []
                    for c in range(NC_):
                        wait(t, "e_sc", silu_v)
                        if c >= 2:
                            wait(t, "e_dve2", tev[c - 2])
                        pt = ptp[c % 2][:]
                        inc(t.transpose(pt, hbcb[:, c * 128:(c + 1) * 128],
                                        identb[:]), "e_pe2")
                        wait(v, "e_pe2")
                        spi = (c % 2) if kind == "B" else (2 + c % 2)
                        if spl_last[spi] is not None:
                            wait(v, *spl_last[spi])
                        sp = spls[spi]
                        if kind == "hid":
                            hh = ordm - 64
                            inc(v.tensor_scalar(sp[:, 0:64], pt[:, 0:64],
                                                dtT[:, c * 128 + 2 * hh: c * 128 + 2 * hh + 1],
                                                None, ALU.mult), "e_dve2")
                            tev.append(inc(v.tensor_scalar(
                                sp[:, 64:128], pt[:, 64:128],
                                dtT[:, c * 128 + 2 * hh + 1: c * 128 + 2 * hh + 2],
                                None, ALU.mult), "e_dve2"))
                        else:
                            tev.append(inc(v.tensor_copy(sp[:], pt), "e_dve2"))
                        wait(s, "e_dve2", tev[c])
                        dstt = (d_hidt[(ordm - 64) * NC_ + c, :, :] if kind == "hid"
                                else d_bt[ordm - 128, c, :, :])
                        psn = "s_sca" if c % 2 == 0 else "s_scb"
                        spf.append((psn, inc(s.dma_start(out=dstt, in_=sp[:]), psn, 16)))
                        spl_last[spi] = spf[-1]
                    scr_free.append(("e_pe2", V["e_pe2"]))
            else:
                for sn, sv in scr_free:
                    wait(s, sn, sv)
                wait(s, "e_dve", evv[mi])
                gg = ordm
                sg_v = inc(s.activation(cscr[:, 0:T], stg[:, HW:W_ALL], AF.Silu), "e_sc")
                stg_free.append(("e_sc", sg_v))
                wait(sy, "e_sc", sg_v)
                for c in range(NCR):
                    inc(sy.dma_start(out=d_sg[gg, c, :, :],
                                     in_=cscr[:, c * 128:(c + 1) * 128]), "s_sy", 16)
                scr_free = [("s_sy", V["s_sy"]), ("e_sc", sg_v)]

        p2_done = dict(sy=V["s_sy"], sc=V["s_sc"], dve=V["e_dve"],
                       pe2=V["e_pe2"], esc=V["e_sc"])

        if debug:
            wait(g, "s_sy", p2_done["sy"])
            wait(g, "s_sc", p2_done["sc"])
            wait(g, "s_sca")
            wait(g, "s_scb")
            wait(g, "e_dve", p2_done["dve"])
            for i in range(4):
                inc(g.dma_start(out=dbg[i, :, 0:T],
                                in_=h_real[:, i * T: i * T + T]), "s_gp", 16)
            inc(g.dma_start(out=dbg[4, :, 0:NC_ * 128], in_=dtT[:]), "s_gp", 16)
            inc(g.dma_start(out=dbg[5, :, 0:NC_ * 128], in_=d_ncat[:]), "s_gp", 16)
            inc(g.dma_start(out=dbg[6, :, 0:NC_], in_=tc_sb[:]), "s_gp", 16)
            inc(g.dma_start(out=dbg[6, :, 32:32 + NC_], in_=etc_sb[:]), "s_gp", 16)
            inc(g.dma_start(out=dbg[7, :, 0:128], in_=d_hidc[0 * NC_ + 2, :, :]), "s_gp", 16)
            inc(g.dma_start(out=dbg[7, :, 128:256], in_=d_b[0, 2, :, :]), "s_gp", 16)
            inc(g.dma_start(out=dbg[7, :, 256:384], in_=d_c[0, 2, :, :]), "s_gp", 16)
            inc(g.dma_start(out=dbg[7, :, 384:512], in_=d_hidt[0 * NC_ + 2, :, :]), "s_gp", 16)
            inc(g.dma_start(out=dbg[7, :, 512:640], in_=d_bt[0, 2, :, :]), "s_gp", 16)
            inc(g.dma_start(out=dbg[8, :, 0:TA], in_=d_chl[0, :, :]), "s_gp", 16)
            inc(g.dma_start(out=dbg[9, :, 0:TA], in_=d_chl[1, :, :]), "s_gp", 16)
            inc(g.dma_start(out=dbg[10, :, 0:128], in_=d_sg[0, 0, :, :]), "s_gp", 16)
            inc(g.dma_start(out=dbg[10, :, 128:256], in_=d_hidc[63 * NC_ + 17, :, :]), "s_gp", 16)
            wait(g, "s_gp")

        if nphase < 3:
            zout = stgs[0][:, 0:T]
            wait(g, "e_sc")
            wait(g, "e_dve")
            inc(g.memset(zout, 0.0), "e_pl")
            wait(sy, "e_pl")
            for nm in ["s_gp", "s_gpa", "s_gpb", "s_sy", "s_sc", "s_sca", "s_scb"]:
                wait(sy, nm)
            for kk in range(KC):
                inc(sy.dma_start(out=out[kk, :, :], in_=zout), "s_sy2", 16)
            wait(sy, "s_sy2")
            free_scope()
            return nc

        wait(g_raw, "e_dve2", V["e_dve2"])
        wait(g_raw, "e_dve", V["e_dve"])
        wait(g_raw, "e_pe", V["e_pe"])
        wait(g_raw, "e_sc", V["e_sc"])
        wait(g_raw, "e_pe2", V["e_pe2"])
        wait(g_raw, "s_sy", V["s_sy"])
        wait(g_raw, "s_sca", V["s_sca"])
        wait(g_raw, "s_scb", V["s_scb"])
        barrier_pl = inc(g_raw.engine_nop(), "e_pl")
        for eng in (t, v_raw, s_raw, sy):
            eng.wait_ge(SEMS["e_pl"], barrier_pl)
        free_scope()

        _emit_p3(nc, V, SEMS, inc, wait, g, v, s, t, sy, g_raw, v_raw, s_raw,
                 sbt_s, pst_s, SB=dict(
                     onescol=onescol, onesrow_f=onesrow_f, onesrow_b=onesrow_b,
                     tc_sb=tc_sb, etc_sb=etc_sb, eps_col=eps_col,
                     dp_sb=dp_sb, gnw_sb=gnw_sb, flag_sb=flag_sb),
                 DR=dict(d_hidt=d_hidt, d_hidc=d_hidc, d_b=d_b, d_bt=d_bt,
                         d_c=d_c, d_chl=d_chl, d_ncat=d_ncat, d_sg=d_sg,
                         d_yn=d_yn, d_bnc=d_bnc), dbg=dbg, debug=debug, p3n=p3n)
        for nm in ["e_pe", "e_dve", "e_sc", "e_sc2", "e_dve2", "e_pe2",
                   "s_sy", "s_yn", "s_cc", "s_h0", "s_h1", "s_c0", "s_c1",
                   "s_m0", "s_m1", "s_g0", "s_g1"]:
            wait(g_raw, nm, V[nm])
        bar3 = inc(g_raw.engine_nop(), "e_pl")
        for eng in (t, v_raw, s_raw, sy):
            eng.wait_ge(SEMS["e_pl"], bar3)
        free_scope()

        if nphase >= 4:
            _emit_p45(nc, V, SEMS, inc, wait, g, v, s, t, sy, g_raw, v_raw,
                      s_raw, sbt_s, pst_s, (free_scope, mark_scope, free_to),
                      SB=dict(onescol=onescol, onesrow_f=onesrow_f,
                              onesrow_b=onesrow_b, eps_col=eps_col, indm=indm),
                      DR=dict(d_yn=d_yn, d_x1=d_x1, d_y1=d_y1, d_int=d_int,
                              d_wd=d_wd, d_x2=d_x2, d_bnc=d_bnc),
                      IO=dict(xin=xin, w2=w2, out=out, gq=gq, gsr=gsr, gz=gz,
                              gso=gso, uq=uq, usr=usr, uz=uz, uso=uso, dq=dq,
                              dsr=dsr, dz=dz, dso=dso),
                      dbg=dbg, debug=debug)
            return nc

        if nphase < 4:
            zo2 = sbt_s("zo2", [128, T], F32)
            wait(g, "e_dve")
            wait(g, "e_sc")
            inc(g.memset(zo2[:], 0.0), "e_pl")
            wait(sy, "e_pl")
            for nm in ["s_gp", "s_gpa", "s_gpb", "s_sy", "s_sc", "s_sca",
                       "s_scb", "s_h0", "s_h1", "s_c0", "s_c1", "s_m0",
                       "s_m1", "s_g0", "s_g1", "s_cc", "s_yn"]:
                wait(sy, nm)
            for kk in range(KC):
                inc(sy.dma_start(out=out[kk, :, :], in_=zo2[:]), "s_sy2", 16)
            wait(sy, "s_sy2")
            free_scope()
            return nc

    return nc


def _emit_p3(nc, V, SEMS, inc, wait, g, v, s, t, sy, g_raw, v_raw, s_raw,
             sbt_s, pst_s, SB, DR, dbg, debug, p3n=64):
    onescol = SB["onescol"]; onesrow_f = SB["onesrow_f"]
    onesrow_b = SB["onesrow_b"]
    tc_sb = SB["tc_sb"]; etc_sb = SB["etc_sb"]; eps_col = SB["eps_col"]
    dp_sb = SB["dp_sb"]; gnw_sb = SB["gnw_sb"]; flag_sb = SB["flag_sb"]
    d_hidt = DR["d_hidt"]; d_hidc = DR["d_hidc"]; d_b = DR["d_b"]
    d_bt = DR["d_bt"]; d_c = DR["d_c"]; d_chl = DR["d_chl"]
    d_ncat = DR["d_ncat"]; d_sg = DR["d_sg"]; d_yn = DR["d_yn"]; d_bnc = DR["d_bnc"]

    y_raw = sbt_s("y_raw", [128, 8 * T], F32)
    ncat_sb = sbt_s("ncat_sb", [128, NC_ * 128], F32)
    TCB = sbt_s("TCB", [128, 128 * NC_], F32)
    EXB = sbt_s("EXB", [128, 128 * NC_], F32)
    RB = [sbt_s("RB0", [128, TA], F32), sbt_s("RB1", [128, TA], F32)]
    _EXR = sbt_s("EXR", [128, TA], BF16)
    EXPROW = [_EXR, _EXR]
    CEXP = [sbt_s("CEX0", [128, T], BF16), sbt_s("CEX1", [128, T], BF16)]
    MT = [sbt_s("MT0", [128, T], BF16), sbt_s("MT1", [128, T], BF16)]
    DIFF = sbt_s("DIFF", [128, T // 2], F32)
    bcache = sbt_s("bcache", [128, NC_ * 128], BF16)
    ccache = sbt_s("ccache", [128, NC_ * 128], BF16)
    btcache = sbt_s("btcache", [128, NC_ * 128], BF16)
    gtm = sbt_s("gtm", [128, NCR * 128], BF16)
    ht_t = [sbt_s("ht0", [128, 128], BF16), sbt_s("ht1", [128, 128], BF16)]
    hc_t = [sbt_s("hc0", [128, 128], BF16), sbt_s("hc1", [128, 128], BF16)]
    hwb = sbt_s("hwb", [128, 128], BF16)
    _sgb = sbt_s("sgb0", [128, T], F32)
    sgb = [_sgb, _sgb]
    S_tot = sbt_s("S_tot", [128, 128], F32)
    S_bf = sbt_s("S_bf", [128, 128], BF16)
    EW = [sbt_s("EW0", [128, NC_], F32), sbt_s("EW1", [128, NC_], F32)]
    rowAB = [sbt_s("rowA", [1, 2 * TA], BF16), sbt_s("rowB", [1, 2 * TA], BF16)]
    grow = sbt_s("grow", [1, 2304], F32)
    GROW = sbt_s("GROW", [128, T], F32)
    sqb = sbt_s("sqb", [128, T], BF16)
    ynb = sqb
    onescol_b2 = sbt_s("onescol_b2", [128, 1], BF16)

    pyb = [pst_s("pyb0", [128, 512]), pst_s("pyb1", [128, 512])]
    pstt = [pst_s("pstt0", [128, 512]), pst_s("pstt1", [128, 512])]
    pgt = pst_s("pgt", [128, 512])
    prb = [pst_s("prb0", [128, 512]), pst_s("prb1", [128, 512])]
    pgn = pst_s("pgn", [128, 512])

    inc(g.memset(onescol_b2[:], 1.0), "e_pl")
    pbank_free = {}

    def bwait(eng, key):
        if key in pbank_free:
            sn, sv = pbank_free[key]
            wait(eng, sn, sv)

    def bset(key, sn):
        pbank_free[key] = (sn, V[sn])
    inc(sy.dma_start(out=ncat_sb[:], in_=d_ncat[:, :]), "s_sy", 16)
    inc(sy.dma_start(out=d_bnc[2, 0:128 * NC_].rearrange("(p c) -> p c", c=NC_),
                     in_=tc_sb[:]), "s_sy", 16)
    inc(sy.dma_start(out=d_bnc[3, 0:128 * NC_].rearrange("(p c) -> p c", c=NC_),
                     in_=etc_sb[:]), "s_sy", 16)
    wait(sy, "s_sy")
    evd2 = []
    for ti, (tbl, rowi) in enumerate(((TCB, 2), (EXB, 3))):
        if ti > 0:
            wait(sy, "e_pe")
        inc(sy.dma_start(out=grow[0:1, 0:2304], in_=d_bnc[rowi:rowi + 1, 0:2304]),
            "s_sy", 16)
        wait(t, "s_sy")
        for i, (off, wd) in enumerate([(0, 512), (512, 512), (1024, 512),
                                       (1536, 512), (2048, 256)]):
            pb = prb[i % 2]
            bwait(t, f"prb{i % 2}")
            inc(t.matmul(pb[:, 0:wd], onesrow_f[:], grow[0:1, off:off + wd],
                         start=True, stop=True), "e_pe")
            wait(s, "e_pe")
            evd2.append(inc(s.copy(tbl[:, off:off + wd], pb[:, 0:wd]), "e_sc2"))
            bset(f"prb{i % 2}", "e_sc2")
    wait(v, "e_sc2")

    py_evac = [0, 0]
    st_upd = [0, 0]
    stmm = [0, 0]
    ht_cons = [(0, 0), (0, 0)]
    hc_cons = [0, 0]
    rb_rows_cons = [0, 0]
    diff_done = [0, 0]
    ymm_done = [0, 0]
    sg_cons = [0, 0]
    gtm_ready = 0
    cache_ld = 0

    for hh in range(p3n):
        gr = hh // 8
        hhl = hh % 8
        if hhl == 0:
            if gr > 0:
                wait(sy, "e_pe")
                wait(sy, "e_dve")
            for cch, src in ((bcache, d_b), (ccache, d_c), (btcache, d_bt)):
                inc(sy.dma_start(out=cch[:].rearrange("p (c f) -> p c f", c=NC_),
                                 in_=src[gr, :, :, :].rearrange("c p f -> p c f")),
                    "s_cc", 16)
            cache_ld = V["s_cc"]
            wait(t, "s_cc", cache_ld)
            gt_ev = []
            for ci in range(NCR):
                c = ci + 2
                if ci >= 1:
                    wait(t, "e_dve", gt_ev[ci - 1])
                bwait(t, "pgt")
                inc(t.matmul(pgt[:, 0:128], bcache[:, c * 128:(c + 1) * 128],
                             ccache[:, c * 128:(c + 1) * 128],
                             start=True, stop=True), "e_pe")
                wait(v, "e_pe")
                gt_ev.append(inc(v.tensor_copy(gtm[:, ci * 128:(ci + 1) * 128],
                                               pgt[:, 0:128]), "e_dve"))
                bset("pgt", "e_dve")
            wait(g, "e_dve", gt_ev[-1])
            for ci in range(NCR):
                inc(g.affine_select(gtm[:, ci * 128:(ci + 1) * 128],
                                    gtm[:, ci * 128:(ci + 1) * 128],
                                    [[1, 128]], ALU.is_ge, 0.0,
                                    base=0, channel_multiplier=-1), "e_pl")
            gtm_ready = V["e_pl"]

        for hi in range(2):
            h = 2 * hh + hi
            pn = "s_m0" if hi == 0 else "s_m1"
            wait(sy, "e_pe", rb_rows_cons[hi])
            rw = rowAB[hi]
            if hh == 0 and hi == 0:
                wait(t, "e_pl")
            inc(sy.dma_start(out=rw[0:1, 0:TA], in_=d_chl[0, h:h + 1, :]), pn, 16)
            inc(sy.dma_start(out=rw[0:1, TA:2 * TA], in_=d_chl[1, h:h + 1, :]), pn, 16)
            rowv = V[pn]
            wait(t, pn, rowv)
            wait(t, "e_sc", diff_done[hi])
            wait(t, "e_dve", max(V["e_dve"] if False else 0, 0))
            for i, (off, wd) in enumerate([(0, 512), (512, 512), (1024, 512),
                                           (1536, 512), (2048, 256)]):
                pb = prb[i % 2]
                bwait(t, f"prb{i % 2}")
                inc(t.matmul(pb[:, 0:wd], onesrow_b[:],
                             rw[0:1, off: off + wd],
                             start=True, stop=False), "e_pe")
                inc(t.matmul(pb[:, 0:wd], onesrow_b[:],
                             rw[0:1, TA + off: TA + off + wd],
                             start=False, stop=True), "e_pe")
                wait(s, "e_pe")
                if i == 0:
                    wait(s, "e_dve", V["e_dve"])
                inc(s.copy(RB[hi][:, off:off + wd], pb[:, 0:wd]), "e_sc2")
                bset(f"prb{i % 2}", "e_sc2")
            rb_rows_cons[hi] = V["e_pe"]
            if hi == 1:
                wait(s, "e_dve", V["e_dve"])
            exprow_v = inc(s.activation(EXPROW[hi][:], RB[hi][:], AF.Exp), "e_sc")
            wait(v, "e_sc", exprow_v)
            wait(v, "s_cc", cache_ld)
            wait(v, "e_pe", ymm_done[hi])
            inc(v.tensor_tensor(CEXP[hi][:], ccache[:, 2 * 128: NC_ * 128],
                                EXPROW[hi][:, 2 * 128: NC_ * 128], ALU.mult), "e_dve")
            inc(v.tensor_add(EW[hi][:], TCB[:, h * NC_:(h + 1) * NC_],
                             ncat_sb[:, h: (NC_ - 1) * 128 + h + 1: 128]), "e_dve")
            ew_add = V["e_dve"]
            wait(s, "e_dve", ew_add)
            inc(s.activation(EW[hi][:], EW[hi][:], AF.Exp), "e_sc")
            for half in range(2):
                wait(v, "e_sc", diff_done[hi])
                for ci in range(half * 8, half * 8 + 8):
                    c = ci + 2
                    inc(v.tensor_scalar(DIFF[:, (ci - half * 8) * 128:(ci - half * 8 + 1) * 128],
                                        RB[hi][:, c * 128:(c + 1) * 128],
                                        ncat_sb[:, c * 128 + h: c * 128 + h + 1],
                                        0.0, ALU.add, ALU.min), "e_dve")
                dv = V["e_dve"]
                wait(s, "e_dve", dv)
                diff_done[hi] = inc(s.activation(MT[hi][:, half * 1024:(half + 1) * 1024],
                                                 DIFF[:], AF.Exp), "e_sc")
            wait(v, "e_sc", diff_done[hi])
            wait(v, "e_pl", gtm_ready)
            inc(v.tensor_tensor(MT[hi][:], MT[hi][:], gtm[:, 0:NCR * 128],
                                ALU.mult), "e_dve")
        mt_cex_ready = V["e_dve"]
        ew_ready = V["e_sc"]

        wait(v, "e_dve")
        inc(v.memset(S_tot[:], 0.0), "e_dve")
        for c in range(NC_):
            p2 = c % 2
            hsem = "s_h0" if p2 == 0 else "s_h1"
            wait(sy, "e_pe", ht_cons[p2][0])
            wait(sy, "e_dve", ht_cons[p2][1])
            inc(sy.dma_start(out=ht_t[p2][:], in_=d_hidt[hh * NC_ + c, :, :]),
                hsem, 16)
            ht_v = V[hsem]
            if c >= 2:
                csem = "s_c0" if p2 == 0 else "s_c1"
                wait(sy, "e_dve", hc_cons[p2])
                inc(sy.dma_start(out=hc_t[p2][:], in_=d_hidc[hh * NC_ + c, :, :]),
                    csem, 16)
                hc_v = V[csem]
            if c >= 2:
                for hi in range(2):
                    inc(v.tensor_copy(S_bf[:, hi * 64:(hi + 1) * 64],
                                      S_tot[:, hi * 64:(hi + 1) * 64]), "e_dve")
                sbf_v = V["e_dve"]
                wait(t, "e_dve", sbf_v)
                wait(t, "e_dve", py_evac[p2] if False else max(py_evac[p2], mt_cex_ready))
                wait(t, hsem, ht_v)
                for hi in range(2):
                    inc(t.matmul(pyb[p2][hi * 64:(hi + 1) * 64, 0:128],
                                 S_bf[:, hi * 64:(hi + 1) * 64],
                                 CEXP[hi][:, (c - 2) * 128:(c - 1) * 128],
                                 start=True, stop=False,
                                 tile_position=(0, 64 * hi)), "e_pe")
                    inc(t.matmul(pyb[p2][hi * 64:(hi + 1) * 64, 0:128],
                                 ht_t[p2][:, hi * 64:(hi + 1) * 64],
                                 MT[hi][:, (c - 2) * 128:(c - 1) * 128],
                                 start=False, stop=True,
                                 tile_position=(0, 64 * hi)), "e_pe")
            ymm_c = V["e_pe"]
            wait(v, hsem, ht_v)
            wait(v, "e_sc", ew_ready)
            wait(v, "e_pe", stmm[1 - p2])
            for hi in range(2):
                h = 2 * hh + hi
                inc(v.tensor_scalar(hwb[:, hi * 64:(hi + 1) * 64],
                                    ht_t[p2][:, hi * 64:(hi + 1) * 64],
                                    EW[hi][:, c:c + 1], None, ALU.mult), "e_dve")
            hw_v = V["e_dve"]
            wait(t, "e_dve", max(hw_v, st_upd[p2]))
            stmm[p2] = inc(t.matmul(pstt[p2][:, 0:128],
                                    btcache[:, c * 128:(c + 1) * 128],
                                    hwb[:], start=True, stop=True), "e_pe")
            wait(v, "e_pe", stmm[p2])
            for hi in range(2):
                h = 2 * hh + hi
                inc(v.scalar_tensor_tensor(
                    S_tot[:, hi * 64:(hi + 1) * 64], S_tot[:, hi * 64:(hi + 1) * 64],
                    EXB[:, h * NC_ + c: h * NC_ + c + 1],
                    pstt[p2][:, hi * 64:(hi + 1) * 64], ALU.mult, ALU.add), "e_dve")
            st_upd[p2] = V["e_dve"]
            if c == 1:
                inc(v.tensor_scalar(S_tot[:], S_tot[:], flag_sb[:, 0:1], None,
                                    ALU.mult), "e_dve")
                st_upd[p2] = V["e_dve"]
            if c >= 2:
                wait(v, "e_pe", ymm_c)
                wait(v, csem, hc_v)
                inc(v.scalar_tensor_tensor(
                    y_raw[:, hhl * T + (c - 2) * 128: hhl * T + (c - 1) * 128],
                    hc_t[p2][:], dp_sb[:, hh:hh + 1], pyb[p2][0:128, 0:128],
                    ALU.mult, ALU.add), "e_dve")
                py_evac[p2] = V["e_dve"]
                hc_cons[p2] = V["e_dve"]
            ht_cons[p2] = (ymm_c, hw_v)
        ymm_done[0] = ymm_done[1] = V["e_pe"]

        if hhl == 7:
            if debug and gr == 0:
                wait(g, "e_dve")
                inc(g.dma_start(out=dbg[12, :, 0:T], in_=y_raw[:, 0:T]), "s_gp", 16)
                wait(v_raw, "s_gp", V["s_gp"])
            for h2 in range(8):
                gg = 8 * gr + h2
                wait(sy, "e_dve", sg_cons[0])
                inc(sy.dma_start(out=sgb[0][:].rearrange("p (c f) -> p c f", c=NCR),
                                 in_=d_sg[gg, :, :, :].rearrange("c p f -> p c f")),
                    "s_g0", 16)
                sg_v = V["s_g0"]
                wait(v, "s_g0", sg_v)
                inc(v.tensor_tensor(y_raw[:, h2 * T:(h2 + 1) * T],
                                    y_raw[:, h2 * T:(h2 + 1) * T],
                                    sgb[0][:], ALU.mult), "e_dve")
                sg_cons[0] = V["e_dve"]
                wait(s, "e_dve", sg_cons[0])
                if h2 > 0:
                    wait(s, "e_pe", V["e_pe"])
                else:
                    wait(s, "s_yn", V["s_yn"])
                inc(s.activation(sqb[:], y_raw[:, h2 * T:(h2 + 1) * T], AF.Square),
                    "e_sc")
                wait(t, "e_sc", V["e_sc"])
                banks = [pgn, prb[0], prb[1], pgt]
                bnames = ["pgn", "prb0", "prb1", "pgt"]
                for ri in range(4):
                    if h2 == 0:
                        bwait(t, bnames[ri])
                    inc(t.matmul(banks[ri][0:1, 0:512], onescol_b2[:],
                                 sqb[:, ri * 512:(ri + 1) * 512],
                                 start=(h2 == 0), stop=(h2 == 7)), "e_pe")
            wait(s, "e_pe")
            banks = [pgn, prb[0], prb[1], pgt]
            bnames = ["pgn", "prb0", "prb1", "pgt"]
            for ri in range(4):
                inc(s.copy(grow[0:1, ri * 512:(ri + 1) * 512],
                           banks[ri][0:1, 0:512]), "e_sc")
                bset(bnames[ri], "e_sc")
            inc(s.activation(grow[0:1, 0:T], grow[0:1, 0:T], AF.Ln,
                             bias=eps_col[0:1, 0:1], scale=1.0 / 1024), "e_sc")
            inc(s.activation(grow[0:1, 0:T], grow[0:1, 0:T], AF.Exp, scale=-0.5),
                "e_sc")
            grv = V["e_sc"]
            wait(t, "e_sc", grv)
            for ri in range(4):
                pb = prb[ri % 2]
                bwait(t, f"prb{ri % 2}")
                inc(t.matmul(pb[:, 0:512], onesrow_f[:],
                             grow[0:1, ri * 512:(ri + 1) * 512],
                             start=True, stop=True), "e_pe")
                wait(s, "e_pe")
                inc(s.copy(GROW[:, ri * 512:(ri + 1) * 512], pb[:, 0:512]), "e_sc2")
                bset(f"prb{ri % 2}", "e_sc2")
            wait(v, "e_sc2")
            yn_sp = 0
            for h2 in range(8):
                gg = 8 * gr + h2
                if h2 > 0:
                    wait(v, "s_yn", yn_sp)
                inc(v.scalar_tensor_tensor(ynb[:], y_raw[:, h2 * T:(h2 + 1) * T],
                                           gnw_sb[:, gg:gg + 1], GROW[:],
                                           ALU.mult, ALU.mult), "e_dve")
                wait(sy, "e_dve")
                inc(sy.dma_start(out=d_yn[gg, :, :], in_=ynb[:]), "s_yn", 16)
                yn_sp = V["s_yn"]
            if debug and gr == 0:
                wait(g, "s_yn", yn_sp)
                inc(g.dma_start(out=dbg[13, :, 0:T], in_=d_yn[0, :, :]), "s_gp", 16)


def _emit_p45(nc, V, SEMS, inc, wait, g, v, s, t, sy, g_raw, v_raw, s_raw,
              sbt_s, pst_s, scope_fns, SB, DR, IO, dbg, debug):
    free_scope, mark_scope, free_to = scope_fns
    onescol = SB["onescol"]; onesrow_f = SB["onesrow_f"]
    onesrow_b = SB["onesrow_b"]; eps_col = SB["eps_col"]; indm = SB["indm"]
    d_yn = DR["d_yn"]; d_x1 = DR["d_x1"]; d_y1 = DR["d_y1"]
    d_bnc = DR["d_bnc"]
    xin = IO["xin"]; w2 = IO["w2"]

    Xs = sbt_s("Xs", [32, T], BF16)
    Xsd = sbt_s("Xsd", [86, T], BF16)
    nzd = sbt_s("nzd", [86, D], BF16)
    mk_base = mark_scope()

    def p4a():
        ynsl = sbt_s("ynsl", [128, 64 * 512], BF16)
        w2t = [sbt_s("w2t0", [128, 64 * 128], BF16),
               sbt_s("w2t1", [128, 64 * 128], BF16)]
        xres = [sbt_s("xres0", [128, 512], F32), sbt_s("xres1", [128, 512], F32)]
        x1b = [sbt_s("x1b0", [128, 512], F32), sbt_s("x1b1", [128, 512], F32)]
        sqx = sbt_s("sqx", [128, 512], BF16)
        oneb = sbt_s("oneb", [128, 1], BF16)
        stmp = sbt_s("stmp", [1, 1024], F32)
        pmm = [pst_s("pm40", [128, 512]), pst_s("pm41", [128, 512])]
        psm = pst_s("psm", [128, 512])
        psq = pst_s("psq", [128, 512])
        inc(g.memset(oneb[:], 1.0), "e_pl")
        w_ld = [None, None]
        mm_done = [0, 0]
        ev_done = [0, 0]
        x1sp = [None, None]
        stat_mm = 0
        stat_ev = 0

        def w2_load(it):
            pn2 = "s_gpa" if it % 2 == 0 else "s_gpb"
            inc(g.dma_start(
                out=w2t[it % 2][:].rearrange("p (k f) -> p k f", k=64),
                in_=w2[it % KC, :].rearrange("(k p f) -> p k f", p=128, k=64)),
                pn2, 16)
            w_ld[it % 2] = (pn2, V[pn2])

        w2_load(0)
        w2_load(1)
        for tt in range(4):
            wait(sy, "e_pe")
            for kk2 in range(64):
                inc(sy.dma_start(out=ynsl[:, kk2 * 512:(kk2 + 1) * 512],
                                 in_=d_yn[kk2, :, tt * 512:(tt + 1) * 512]),
                    "s_cc", 16)
            slab_v = V["s_cc"]
            for m in range(KC):
                it = tt * KC + m
                bi = it % 2
                pxn = "s_m0" if bi == 0 else "s_m1"
                wait(sy, "e_dve", ev_done[bi])
                inc(sy.dma_start(out=xres[bi][:],
                                 in_=xin[m, :, HW + tt * 512: HW + (tt + 1) * 512]),
                    pxn, 16)
                x_ld = (pxn, V[pxn])
                wait(t, *w_ld[bi])
                wait(t, "s_cc", slab_v)
                wait(t, "e_dve", ev_done[bi])
                for k in range(64):
                    ins = t.matmul(pmm[bi][:, 0:512],
                                   w2t[bi][:, k * 128:(k + 1) * 128],
                                   ynsl[:, k * 512:(k + 1) * 512],
                                   start=(k == 0), stop=(k == 63))
                mm_done[bi] = inc(ins, "e_pe")
                if it + 2 < 128:
                    wait(g, "e_pe", mm_done[bi])
                    w2_load(it + 2)
                wait(v, "e_pe", mm_done[bi])
                wait(v, *x_ld)
                if x1sp[bi] is not None:
                    wait(v, *x1sp[bi])
                wait(v, "e_pe", stat_mm)
                inc(v.scalar_tensor_tensor(x1b[bi][:], xres[bi][:], 2.0,
                                           pmm[bi][:, 0:512], ALU.mult, ALU.add),
                    "e_dve")
                ev_done[bi] = V["e_dve"]
                wait(sy, "e_dve", ev_done[bi])
                psp = "s_h0" if bi == 0 else "s_h1"
                inc(sy.dma_start(out=d_x1[m, :, tt * 512:(tt + 1) * 512],
                                 in_=x1b[bi][:]), psp, 16)
                x1sp[bi] = (psp, V[psp])
                wait(s, "e_dve", ev_done[bi])
                wait(s, "e_pe", stat_mm)
                inc(s.activation(sqx[:], x1b[bi][:], AF.Square), "e_sc")
                sqv = V["e_sc"]
                wait(t, "e_sc", sqv)
                if m == 0:
                    wait(t, "e_sc2", stat_ev)
                t.matmul(psm[0:1, 0:512], onescol[:], x1b[bi][:],
                         start=(m == 0), stop=(m == KC - 1))
                ins = t.matmul(psq[0:1, 0:512], oneb[:], sqx[:],
                               start=(m == 0), stop=(m == KC - 1))
                stat_mm = inc(ins, "e_pe")
            wait(s, "e_pe", stat_mm)
            wait(s, "s_sc", V["s_sc"])
            inc(s.copy(stmp[0:1, 0:512], psm[0:1, 0:512]), "e_sc2")
            inc(s.copy(stmp[0:1, 512:1024], psq[0:1, 0:512]), "e_sc2")
            stat_ev = V["e_sc2"]
            wait(sy, "e_sc2", stat_ev)
            inc(sy.dma_start(out=d_bnc[4, tt * 512:(tt + 1) * 512],
                             in_=stmp[0:1, 0:512]), "s_sc", 16)
            inc(sy.dma_start(out=d_bnc[5, tt * 512:(tt + 1) * 512],
                             in_=stmp[0:1, 512:1024]), "s_sc", 16)

    p4a()
    for nm in ["e_pe", "e_dve", "e_sc", "e_sc2", "s_cc", "s_m0", "s_m1",
               "s_h0", "s_h1", "s_gpa", "s_gpb"]:
        wait(g_raw, nm, V[nm])
    bar = inc(g_raw.engine_nop(), "e_pl")
    for eng in (t, v_raw, s_raw, sy):
        eng.wait_ge(SEMS["e_pl"], bar)
    free_to(mk_base)
    y1res = sbt_s("y1res", [128, KC * T], BF16)
    mk_y1 = mark_scope()

    def p4b():
        MUR = sbt_s("MUR", [128, T], F32)
        RSR = sbt_s("RSR", [128, T], F32)
        mu_row = sbt_s("mu_row", [1, T], F32)
        vrow = sbt_s("vrow", [1, T], F32)
        wait(sy, "s_sc", V["s_sc"])
        inc(sy.dma_start(out=mu_row[0:1, :], in_=d_bnc[4:5, 0:T]), "s_sy", 16)
        inc(sy.dma_start(out=vrow[0:1, :], in_=d_bnc[5:6, 0:T]), "s_sy", 16)
        wait(v, "s_sy", V["s_sy"])
        x1rb = [sbt_s("x1rb0", [128, T], F32), sbt_s("x1rb1", [128, T], F32)]
        pr4 = [pst_s("pr40", [128, 512]), pst_s("pr41", [128, 512])]

        inc(v.tensor_single_scalar(mu_row[0:1, :], mu_row[0:1, :], 1.0 / D,
                                   ALU.mult), "e_dve")
        inc(v.tensor_single_scalar(vrow[0:1, :], vrow[0:1, :], 1.0 / D,
                                   ALU.mult), "e_dve")
        inc(v.tensor_tensor(MUR[0:1, :], mu_row[0:1, :], mu_row[0:1, :],
                            ALU.mult), "e_dve")
        inc(v.tensor_sub(vrow[0:1, :], vrow[0:1, :], MUR[0:1, :]), "e_dve")
        vv = V["e_dve"]
        wait(s, "e_dve", vv)
        inc(s.activation(vrow[0:1, :], vrow[0:1, :], AF.Ln,
                         bias=eps_col[0:1, 0:1]), "e_sc")
        inc(s.activation(vrow[0:1, :], vrow[0:1, :], AF.Exp, scale=-0.5), "e_sc")
        rsv = V["e_sc"]
        wait(t, "e_sc", rsv)
        pb_ev = [0, 0]
        for ri in range(8):
            src, dst = (mu_row, MUR) if ri < 4 else (vrow, RSR)
            off = (ri % 4) * 512
            pb = pr4[ri % 2]
            wait(t, "e_sc2", pb_ev[ri % 2])
            inc(t.matmul(pb[:, 0:512], onesrow_f[:], src[0:1, off:off + 512],
                         start=True, stop=True), "e_pe")
            wait(s, "e_pe")
            inc(s.copy(dst[:, off:off + 512], pb[:, 0:512]), "e_sc2")
            pb_ev[ri % 2] = V["e_sc2"]
        wait(v, "e_sc2")
        x1_ld = [None, None]
        ap_done = [0, 0]
        y1sp = [None, None]
        for m in range(KC):
            bi = m % 2
            pxn = "s_m0" if bi == 0 else "s_m1"
            wait(sy, "e_dve", ap_done[bi])
            if y1sp[bi] is not None:
                wait(sy, *y1sp[bi])
            inc(sy.dma_start(out=x1rb[bi][:], in_=d_x1[m, :, :]), pxn, 16)
            x1_ld[bi] = (pxn, V[pxn])
            wait(v, *x1_ld[bi])
            inc(v.tensor_sub(x1rb[bi][:], x1rb[bi][:], MUR[:]), "e_dve")
            inc(v.tensor_tensor(x1rb[bi][:], x1rb[bi][:], RSR[:], ALU.mult),
                "e_dve")
            inc(v.tensor_copy(y1res[:, m * T:(m + 1) * T], x1rb[bi][:]), "e_dve")
            ap_done[bi] = V["e_dve"]
            wait(sy, "e_dve", ap_done[bi])
            psp = "s_h0" if bi == 0 else "s_h1"
            inc(sy.dma_start(out=d_y1[m, :, :], in_=x1rb[bi][:]), psp, 16)
            y1sp[bi] = (psp, V[psp])

    p4b()
    if debug:
        wait(g, "e_dve", V["e_dve"])
        inc(g.dma_start(out=dbg[14, :, 0:T], in_=y1res[:, 0:T]), "s_gp", 16)
        wait(g_raw, "s_gp", V["s_gp"])
    for nm in ["e_pe", "e_dve", "e_sc", "e_sc2", "s_m0", "s_m1", "s_h0", "s_h1"]:
        wait(g_raw, nm, V[nm])
    bar = inc(g_raw.engine_nop(), "e_pl")
    for eng in (t, v_raw, s_raw, sy):
        eng.wait_ge(SEMS["e_pl"], bar)
    free_to(mk_y1)
    _emit_p5(nc, V, SEMS, inc, wait, g, v, s, t, sy, g_raw, v_raw, s_raw,
             sbt_s, pst_s, (free_scope, mark_scope, free_to, mk_base, mk_y1),
             y1res, Xs, Xsd, nzd, SB, DR, IO, dbg, debug)


def _emit_p5(nc, V, SEMS, inc, wait, g, v, s, t, sy, g_raw, v_raw, s_raw,
             sbt_s, pst_s, scope_fns, y1res, Xs, Xsd, nzd, SB, DR, IO, dbg,
             debug):
    free_scope, mark_scope, free_to, mk_base, mk_y1 = scope_fns
    onescol = SB["onescol"]; onesrow_f = SB["onesrow_f"]
    onesrow_b = SB["onesrow_b"]; eps_col = SB["eps_col"]; indm = SB["indm"]
    d_int = DR["d_int"]; d_wd = DR["d_wd"]; d_x2 = DR["d_x2"]; d_y1 = DR["d_y1"]
    out = IO["out"]
    gq = IO["gq"]; gsr = IO["gsr"]; gz = IO["gz"]; gso = IO["gso"]
    uq = IO["uq"]; usr = IO["usr"]; uz = IO["uz"]; uso = IO["uso"]
    dq = IO["dq"]; dsr = IO["dsr"]; dz = IO["dz"]; dso = IO["dso"]

    d_nzp = nc.dram_tensor("d_nzp", [2, 32, INTP], BF16, kind="Internal").ap()

    def p5a():
        qb = [sbt_s("qb0", [128, 512], I32), sbt_s("qb1", [128, 512], I32)]
        sb_ = [sbt_s("sb0", [128, 512], F16), sbt_s("sb1", [128, 512], F16)]
        wgu = [sbt_s("wg0", [128, KC * 128], BF16),
               sbt_s("wg1", [128, KC * 128], BF16)]
        uscr = sbt_s("uscr", [128, 512], I32)
        sgate = sbt_s("sgate", [128, 512], F32)
        intb = [sbt_s("intb0", [128, 512], BF16), sbt_s("intb1", [128, 512], BF16)]
        nzt = [sbt_s("nzt0", [32, 128], BF16), sbt_s("nzt1", [32, 128], BF16)]
        zint = sbt_s("zint", [86, 1408], I32)
        zf = sbt_s("zf", [86, 1408], F32)
        zsc = sbt_s("zsc", [86, 1408], F16)
        zbf = sbt_s("zbf", [86, 1408], BF16)
        pgu = [pst_s("pgu0", [128, 512]), pst_s("pgu1", [128, 512])]
        pxs = [pst_s(f"pxs{i}", [128, 512]) for i in range(4)]

        zw = 0
        for side, (zz, sso) in enumerate(((gz, gso), (uz, uso))):
            wait(sy, "e_dve", zw)
            inc(sy.dma_start(out=zint[0:32, 0:1376], in_=zz[:, :]), "s_sy", 16)
            for j in range(8):
                wait(sy, "e_dve", zw)
                inc(sy.dma_start(out=zsc[0:32, 0:1376],
                                 in_=sso[:, j * 1408: j * 1408 + 1376]), "s_sy", 16)
                wait(v, "s_sy")
                inc(v.tensor_scalar(zf[0:32, 0:1376].bitcast(I32), zint[0:32, 0:1376],
                                    4 * j, 15, ALU.logical_shift_right,
                                    ALU.bitwise_and), "e_dve")
                inc(v.tensor_scalar(zf[0:32, 0:1376], zf[0:32, 0:1376].bitcast(I32),
                                    -1.0, -1.0, ALU.mult, ALU.add), "e_dve")
                inc(v.tensor_tensor(zbf[0:32, 0:1376], zf[0:32, 0:1376],
                                    zsc[0:32, 0:1376], ALU.mult), "e_dve")
                zw = V["e_dve"]
                wait(sy, "e_dve", zw)
                inc(sy.dma_start(out=d_nzp[side, :, j * 1408: j * 1408 + 1376],
                                 in_=zbf[0:32, 0:1376]), "s_sy", 16)
        wait(v, "s_sy")
        inc(v.memset(zbf[0:32, 0:32], 0.0), "e_dve")
        wait(sy, "e_dve")
        for side in range(2):
            for j in range(8):
                inc(sy.dma_start(out=d_nzp[side, :, j * 1408 + 1376:(j + 1) * 1408],
                                 in_=zbf[0:32, 0:32]), "s_sy", 16)

        wait(t, "e_pl")
        for cc in range(KC):
            c4 = cc % 4
            for tt in range(4):
                ins = t.matmul(pxs[tt][0:32, 0:512],
                               indm[:, 80 - 8 * c4: 80 - 8 * c4 + 32],
                               y1res[:, cc * T + tt * 512: cc * T + (tt + 1) * 512],
                               start=(cc == 0), stop=(cc == KC - 1))
            inc(ins, "e_pe")
        wait(v, "e_pe")
        for tt in range(4):
            inc(v.tensor_copy(Xs[:, tt * 512:(tt + 1) * 512],
                              pxs[tt][0:32, 0:512]), "e_dve")
        xs_v = V["e_dve"]

        wait(sy, "s_sy", V["s_sy"])
        deq_done = [0, 0]
        mm_gu = [0, 0]
        int_sp = [None, None]
        xsd_mm = 0
        pgu_free = [("e_sc", 0), ("e_dve", 0)]
        for b in range(NB):
            c11 = b % 11
            for side, (qq, ss) in enumerate(((gq, gsr), (uq, usr))):
                pn = "s_h0" if side == 0 else "s_h1"
                wait(sy, "e_dve", deq_done[side])
                inc(sy.dma_start(
                    out=qb[side][:].rearrange("p (c f) -> p c f", c=4),
                    in_=qq.rearrange("(c p) n -> p c n", p=128)[:, :, b * 128:(b + 1) * 128]),
                    pn, 16)
                inc(sy.dma_start(
                    out=sb_[side][:].rearrange("p (c f) -> p c f", c=4),
                    in_=ss.rearrange("(c p) n -> p c n", p=128)[:, :, b * 128:(b + 1) * 128]),
                    pn, 16)
                q_ld = (pn, V[pn])
                pnz = "s_m0" if side == 0 else "s_m1"
                wait(sy, "e_pe", mm_gu[side])
                inc(sy.dma_start(out=nzt[side][:],
                                 in_=d_nzp[side, :, b * 128:(b + 1) * 128]),
                    pnz, 16)
                nz_ld = (pnz, V[pnz])
                wait(v, *q_ld)
                wait(v, "e_pe", mm_gu[side])
                for j in range(8):
                    inc(v.tensor_scalar(uscr[:], qb[side][:], 4 * j, 15,
                                        ALU.logical_shift_right, ALU.bitwise_and),
                        "e_dve")
                    inc(v.tensor_tensor(wgu[side][:, j * 512:(j + 1) * 512],
                                        uscr[:], sb_[side][:], ALU.mult), "e_dve")
                deq_done[side] = V["e_dve"]
                if side == 0:
                    nz_g = nz_ld
                else:
                    nz_u = nz_ld
            for tt in range(4):
                for side in range(2):
                    wait(t, "e_dve", deq_done[side])
                    wait(t, *(nz_g if side == 0 else nz_u))
                    wait(t, "e_dve", xs_v)
                    sn, svv = pgu_free[side]
                    wait(t, sn, svv)
                    pp = pgu[side]
                    t.matmul(pp[:, 0:512], nzt[side][:],
                             Xs[:, tt * 512:(tt + 1) * 512],
                             start=True, stop=False)
                    for kkc in range(KC):
                        ins = t.matmul(
                            pp[:, 0:512], wgu[side][:, kkc * 128:(kkc + 1) * 128],
                            y1res[:, kkc * T + tt * 512: kkc * T + (tt + 1) * 512],
                            start=False, stop=(kkc == KC - 1))
                    mm_gu[side] = inc(ins, "e_pe")
                wait(s, "e_pe", mm_gu[1])
                inc(s.activation(sgate[:], pgu[0][:, 0:512], AF.Silu), "e_sc")
                sgv = V["e_sc"]
                pgu_free[0] = ("e_sc", sgv)
                ib = intb[tt % 2]
                wait(v, "e_sc", sgv)
                if int_sp[tt % 2] is not None:
                    wait(v, *int_sp[tt % 2])
                wait(v, "e_pe", xsd_mm)
                inc(v.tensor_tensor(ib[:], sgate[:], pgu[1][:, 0:512], ALU.mult),
                    "e_dve")
                ibv = V["e_dve"]
                pgu_free[1] = ("e_dve", ibv)
                wait(sy, "e_dve", ibv)
                pis = "s_c0" if tt % 2 == 0 else "s_c1"
                inc(sy.dma_start(out=d_int[b, :, tt * 512:(tt + 1) * 512],
                                 in_=ib[:]), pis, 16)
                int_sp[tt % 2] = (pis, V[pis])
                wait(t, "e_dve", ibv)
                ins = t.matmul(pxs[tt][0:86, 0:512],
                               indm[:, 80 - 8 * c11: 80 - 8 * c11 + 86],
                               ib[:], start=(b == 0), stop=(b == NB - 1))
                xsd_mm = inc(ins, "e_pe")
        wait(v, "e_pe", xsd_mm)
        for tt in range(4):
            inc(v.tensor_copy(Xsd[:, tt * 512:(tt + 1) * 512],
                              pxs[tt][0:86, 0:512]), "e_dve")
        zw2 = V["e_dve"]
        wait(sy, "e_dve", zw2)
        inc(sy.dma_start(out=zint[0:86, 0:512], in_=dz[:, :]), "s_sy", 16)
        for j in range(8):
            wait(sy, "e_dve", zw2)
            inc(sy.dma_start(out=zsc[0:86, 0:512],
                             in_=dso[:, j * 512:(j + 1) * 512]), "s_sy", 16)
            wait(v, "s_sy")
            inc(v.tensor_scalar(zf[0:86, 0:512].bitcast(I32), zint[0:86, 0:512],
                                4 * j, 15, ALU.logical_shift_right,
                                ALU.bitwise_and), "e_dve")
            inc(v.tensor_scalar(zf[0:86, 0:512], zf[0:86, 0:512].bitcast(I32),
                                -1.0, -1.0, ALU.mult, ALU.add), "e_dve")
            inc(v.tensor_tensor(nzd[0:86, j * 512:(j + 1) * 512], zf[0:86, 0:512],
                                zsc[0:86, 0:512], ALU.mult), "e_dve")
            zw2 = V["e_dve"]

    p5a()
    for nm in ["e_pe", "e_dve", "e_sc", "s_sy", "s_yn", "s_h0", "s_h1",
               "s_m0", "s_m1", "s_c0", "s_c1"]:
        wait(g_raw, nm, V[nm])
    bar = inc(g_raw.engine_nop(), "e_pl")
    for eng in (t, v_raw, s_raw, sy):
        eng.wait_ge(SEMS["e_pl"], bar)
    free_to(mk_base)

    def p5b():
        dqb = [sbt_s("dqb0", [128, D], I32), sbt_s("dqb1", [128, D], I32)]
        dsb = [sbt_s("dsb0", [128, D], F16), sbt_s("dsb1", [128, D], F16)]
        uscr2 = sbt_s("uscr2", [128, D], I32)
        wdb = [sbt_s("wdb0", [128, D], BF16), sbt_s("wdb1", [128, D], BF16)]
        dq_ld = [None, None]
        deq = [0, 0]
        wsp = [None, None]
        for c in range(11):
            bi = c % 2
            pn = "s_h0" if bi == 0 else "s_h1"
            wait(sy, "e_dve", deq[bi])
            inc(sy.dma_start(out=dqb[bi][:], in_=dq[c * 128:(c + 1) * 128, :]),
                pn, 16)
            inc(sy.dma_start(out=dsb[bi][:], in_=dsr[c * 128:(c + 1) * 128, :]),
                pn, 16)
            dq_ld[bi] = (pn, V[pn])
            wait(v, *dq_ld[bi])
            for j in range(8):
                inc(v.tensor_scalar(uscr2[:], dqb[bi][:], 4 * j, 15,
                                    ALU.logical_shift_right, ALU.bitwise_and),
                    "e_dve")
                wbi = j % 2
                if wsp[wbi] is not None:
                    wait(v, *wsp[wbi])
                inc(v.tensor_tensor(wdb[wbi][:], uscr2[:], dsb[bi][:], ALU.mult),
                    "e_dve")
                wv = V["e_dve"]
                wait(sy, "e_dve", wv)
                psp = "s_m0" if wbi == 0 else "s_m1"
                inc(sy.dma_start(out=d_wd[j * 11 + c, :, :], in_=wdb[wbi][:]),
                    psp, 16)
                wsp[wbi] = (psp, V[psp])
            deq[bi] = V["e_dve"]

    mk5 = mark_scope()
    p5b()
    for nm in ["e_dve", "s_m0", "s_m1", "s_h0", "s_h1"]:
        wait(g_raw, nm, V[nm])
    bar2 = inc(g_raw.engine_nop(), "e_pl")
    for eng in (t, v_raw, s_raw, sy):
        eng.wait_ge(SEMS["e_pl"], bar2)
    free_to(mk5)

    mu2 = sbt_s("mu2", [1, T], F32)
    m22 = sbt_s("m22", [1, T], F32)
    mk5c = mark_scope()

    def p5c():
        islab = sbt_s("islab", [128, NB * 512], BF16)
        wds = [sbt_s(f"wds{i}", [128, 512], BF16) for i in range(4)]
        f16t = sbt_s("f16t", [128, 512], F16)
        y1rb = [sbt_s("y1rb0", [128, 512], F32), sbt_s("y1rb1", [128, 512], F32)]
        x2b = [sbt_s("x2b0", [128, 512], F32), sbt_s("x2b1", [128, 512], F32)]
        sqx2 = sbt_s("sqx2", [128, 512], BF16)
        oneb2 = sbt_s("oneb2", [128, 1], BF16)
        pdn = [pst_s(f"pdn{i}", [128, 512]) for i in range(4)]
        psm2 = pst_s("psm2", [128, 512])
        psq2 = pst_s("psq2", [128, 512])
        inc(g.memset(oneb2[:], 1.0), "e_pl")
        wait(t, "e_pl")
        w_ld = [None, None, None, None]
        wds_mm = [0, 0, 0, 0]
        WSEM = ["s_h0", "s_h1", "s_c0", "s_c1"]
        ev2 = [0, 0, 0, 0]
        x2sp = [None, None]
        stat_mm = 0
        stat_ev = 0
        evb = [0, 0]
        for tt in range(4):
            wait(sy, "e_pe")
            for b in range(NB):
                inc(sy.dma_start(out=islab[:, b * 512:(b + 1) * 512],
                                 in_=d_int[b, :, tt * 512:(tt + 1) * 512]),
                    "s_cc", 16)
            slab_v = V["s_cc"]
            for mg in range(8):
                for mm4 in range(4):
                    m = mg * 4 + mm4
                    wait(t, "e_dve", ev2[mm4])
                    wait(t, "s_cc", slab_v)
                    ins = t.matmul(pdn[mm4][:, 0:512], nzd[:, m * 128:(m + 1) * 128],
                                   Xsd[:, tt * 512:(tt + 1) * 512],
                                   start=True, stop=False)
                def wds_load(kb):
                    bi = kb % 4
                    pn = WSEM[bi]
                    wait(sy, "e_pe", wds_mm[bi])
                    inc(sy.dma_start(out=wds[bi][:],
                                     in_=d_wd[kb, :, mg * 512:(mg + 1) * 512]),
                        pn, 16)
                    w_ld[bi] = (pn, V[pn])

                for kb in range(3):
                    wds_load(kb)
                for kb in range(NB):
                    kp = 96 if kb % 11 == 10 else 128
                    bi = kb % 4
                    if kb + 3 < NB:
                        wds_load(kb + 3)
                    wait(t, *w_ld[bi])
                    for mm4 in range(4):
                        ins = t.matmul(pdn[mm4][:, 0:512],
                                       wds[bi][0:kp, mm4 * 128:(mm4 + 1) * 128],
                                       islab[0:kp, kb * 512:(kb + 1) * 512],
                                       start=False, stop=(kb == NB - 1))
                    wds_mm[bi] = inc(ins, "e_pe")
                mmv = V["e_pe"]
                for mm4 in range(4):
                    m = mg * 4 + mm4
                    bi = m % 2
                    pxn = "s_m0" if bi == 0 else "s_m1"
                    wait(sy, "e_dve", evb[bi])
                    inc(sy.dma_start(out=y1rb[bi][:],
                                     in_=d_y1[m, :, tt * 512:(tt + 1) * 512]),
                        pxn, 16)
                    yl = (pxn, V[pxn])
                    wait(v, "e_pe", mmv)
                    wait(v, *yl)
                    if x2sp[bi] is not None:
                        wait(v, *x2sp[bi])
                    wait(v, "e_pe", stat_mm)
                    inc(v.tensor_copy(f16t[:], pdn[mm4][:, 0:512]), "e_dve")
                    inc(v.tensor_add(x2b[bi][:], f16t[:], y1rb[bi][:]), "e_dve")
                    ev2[mm4] = V["e_dve"]
                    evb[bi] = V["e_dve"]
                    wait(sy, "e_dve", evb[bi])
                    psp = "s_g0" if bi == 0 else "s_g1"
                    inc(sy.dma_start(out=d_x2[m, :, tt * 512:(tt + 1) * 512],
                                     in_=x2b[bi][:]), psp, 16)
                    x2sp[bi] = (psp, V[psp])
                    wait(s, "e_dve", evb[bi])
                    wait(s, "e_pe", stat_mm)
                    inc(s.activation(sqx2[:], x2b[bi][:], AF.Square), "e_sc")
                    sqv = V["e_sc"]
                    wait(t, "e_sc", sqv)
                    if tt == 0 and mg == 0 and mm4 == 0:
                        pass
                    if mg == 0 and mm4 == 0:
                        wait(t, "e_sc2", stat_ev)
                    t.matmul(psm2[0:1, 0:512], onescol[:], x2b[bi][:],
                             start=(mg == 0 and mm4 == 0),
                             stop=(mg == 7 and mm4 == 3))
                    ins = t.matmul(psq2[0:1, 0:512], oneb2[:], sqx2[:],
                                   start=(mg == 0 and mm4 == 0),
                                   stop=(mg == 7 and mm4 == 3))
                    stat_mm = inc(ins, "e_pe")
            wait(s, "e_pe", stat_mm)
            inc(s.copy(mu2[0:1, tt * 512:(tt + 1) * 512], psm2[0:1, 0:512]),
                "e_sc2")
            inc(s.copy(m22[0:1, tt * 512:(tt + 1) * 512], psq2[0:1, 0:512]),
                "e_sc2")
            stat_ev = V["e_sc2"]

    p5c()
    for nm in ["e_pe", "e_dve", "e_sc", "e_sc2", "s_cc", "s_m0", "s_m1",
               "s_h0", "s_h1", "s_g0", "s_g1"]:
        wait(g_raw, nm, V[nm])
    bar3 = inc(g_raw.engine_nop(), "e_pl")
    for eng in (t, v_raw, s_raw, sy):
        eng.wait_ge(SEMS["e_pl"], bar3)
    free_to(mk5c)

    def p5d():
        MUR2 = sbt_s("MUR2", [128, T], F32)
        RSR2 = sbt_s("RSR2", [128, T], F32)
        vrow2 = sbt_s("vrow2", [1, T], F32)
        x2rb = [sbt_s("x2rb0", [128, T], F32), sbt_s("x2rb1", [128, T], F32)]
        orb = [sbt_s("orb0", [128, T], F32), sbt_s("orb1", [128, T], F32)]
        pr5 = [pst_s("pr50", [128, 512]), pst_s("pr51", [128, 512])]
        inc(v.tensor_single_scalar(mu2[0:1, :], mu2[0:1, :], 1.0 / D, ALU.mult),
            "e_dve")
        inc(v.tensor_single_scalar(m22[0:1, :], m22[0:1, :], 1.0 / D, ALU.mult),
            "e_dve")
        inc(v.tensor_tensor(vrow2[0:1, :], mu2[0:1, :], mu2[0:1, :], ALU.mult),
            "e_dve")
        inc(v.tensor_sub(vrow2[0:1, :], m22[0:1, :], vrow2[0:1, :]), "e_dve")
        vv = V["e_dve"]
        wait(s, "e_dve", vv)
        inc(s.activation(vrow2[0:1, :], vrow2[0:1, :], AF.Ln,
                         bias=eps_col[0:1, 0:1]), "e_sc")
        inc(s.activation(vrow2[0:1, :], vrow2[0:1, :], AF.Exp, scale=-0.5),
            "e_sc")
        rsv = V["e_sc"]
        wait(t, "e_sc", rsv)
        pb_ev = [0, 0]
        for ri in range(8):
            src, dst = (mu2, MUR2) if ri < 4 else (vrow2, RSR2)
            off = (ri % 4) * 512
            pb = pr5[ri % 2]
            wait(t, "e_sc2", pb_ev[ri % 2])
            inc(t.matmul(pb[:, 0:512], onesrow_f[:], src[0:1, off:off + 512],
                         start=True, stop=True), "e_pe")
            wait(s, "e_pe")
            inc(s.copy(dst[:, off:off + 512], pb[:, 0:512]), "e_sc2")
            pb_ev[ri % 2] = V["e_sc2"]
        wait(v, "e_sc2")
        ap = [0, 0]
        osp = [None, None]
        for m in range(KC):
            bi = m % 2
            pxn = "s_m0" if bi == 0 else "s_m1"
            wait(sy, "e_dve", ap[bi])
            inc(sy.dma_start(out=x2rb[bi][:], in_=d_x2[m, :, :]), pxn, 16)
            xl = (pxn, V[pxn])
            wait(v, *xl)
            if osp[bi] is not None:
                wait(v, *osp[bi])
            inc(v.tensor_sub(x2rb[bi][:], x2rb[bi][:], MUR2[:]), "e_dve")
            inc(v.tensor_tensor(orb[bi][:], x2rb[bi][:], RSR2[:], ALU.mult),
                "e_dve")
            ap[bi] = V["e_dve"]
            wait(sy, "e_dve", ap[bi])
            pso = "s_g0" if bi == 0 else "s_g1"
            inc(sy.dma_start(out=out[m, :, :], in_=orb[bi][:]), pso, 16)
            osp[bi] = (pso, V[pso])

    p5d()
    for nm in ["s_gp", "s_gpa", "s_gpb", "s_sy", "s_sc", "s_sca", "s_scb",
               "s_h0", "s_h1", "s_c0", "s_c1", "s_m0", "s_m1", "s_g0", "s_g1",
               "s_cc", "s_yn", "s_sy2"]:
        wait(sy, nm)
    free_scope()


def _dt_path(nc, V, SEMS, inc, wait, g, v, s, t, sy, stg, stg1, cscr, hbcb, dtT,
             tc_sb, etc_sb, acol, dtb_sb, identf, ptf, zeros128, d_chl, d_ncat,
             spls, onescol):
    if True:
        t0 = stg1[:, 0:TA]
        hl = hbcb[:, 0:TA]
        wait(v, "e_dve")
        vstg = inc(v.tensor_scalar(t0, stg[:, PADC:W_ALL], dtb_sb[:, 0:1], None,
                                   ALU.add), "e_dve")
        wait(s, "e_dve", vstg)
        vabs = inc(s.activation(cscr[:], t0, AF.Abs), "e_sc")
        inc(s.activation(cscr[:], cscr[:], AF.Exp, scale=-1.0), "e_sc")
        inc(s.activation(cscr[:], cscr[:], AF.Ln, bias=onescol[:, 0:1]), "e_sc")
        wait(v, "e_sc", vabs)
        inc(v.tensor_single_scalar(t0, t0, 0.0, ALU.max), "e_dve")
        wait(v, "e_sc")
        inc(v.tensor_add(t0, t0, cscr[:]), "e_dve")
        tev = []
        for c in range(NC_):
            wait(t, "e_dve")
            if c >= 1:
                wait(t, "e_dve2", tev[c - 1])
            inc(t.transpose(ptf[:], t0[:, c * 128:(c + 1) * 128], identf[:]), "e_pe2")
            wait(v, "e_pe2")
            tev.append(inc(v.tensor_copy(dtT[:, c * 128:(c + 1) * 128], ptf[:]),
                           "e_dve2"))
        inc(v.tensor_scalar(cscr[:], t0, acol[:, 0:1], None, ALU.mult), "e_dve")
        wait(v, "e_dve")
        for c in range(NC_):
            inc(v.tensor_tensor_scan(t0[:, c * 128:(c + 1) * 128],
                                     cscr[:, c * 128:(c + 1) * 128],
                                     zeros128[:], 0.0, ALU.add, ALU.add), "e_dve")
        wait(v, "e_dve")
        inc(v.tensor_copy(tc_sb[:], t0[:, 127::128]), "e_dve")
        wait(s, "e_dve")
        inc(s.activation(etc_sb[:], tc_sb[:], AF.Exp), "e_sc")
        inc(v.tensor_copy(hl, t0), "e_dve")
        wait(v, "e_dve")
        inc(v.tensor_sub(cscr[:], t0, hl), "e_dve")
        wait(sy, "e_dve")
        inc(sy.dma_start(out=d_chl[0, :, :], in_=hl), "s_sy", 16)
        wait(v, "s_sy")
        inc(v.tensor_copy(hl, cscr[:]), "e_dve")
        wait(sy, "e_dve")
        inc(sy.dma_start(out=d_chl[1, :, :], in_=hl), "s_sy", 16)
        inc(v.tensor_single_scalar(t0, t0, -1.0, ALU.mult), "e_dve")
        spf = []
        for c in range(NC_):
            wait(t, "e_dve")
            if c >= 1:
                wait(t, "e_dve2", V["e_dve2"])
            inc(t.transpose(ptf[:], t0[:, c * 128:(c + 1) * 128], identf[:]), "e_pe2")
            wait(v, "e_pe2")
            if c >= 2:
                wait(v, *spf[c - 2])
            sp = spls[c % 2]
            inc(v.tensor_copy(cscr[:, (c % 2) * 128:(c % 2) * 128 + 128], ptf[:]),
                "e_dve2")
            wait(sy, "e_dve2")
            psn = "s_sca" if c % 2 == 0 else "s_scb"
            spf.append((psn, inc(sy.dma_start(
                out=d_ncat[:, c * 128:(c + 1) * 128],
                in_=cscr[:, (c % 2) * 128:(c % 2) * 128 + 128]), psn, 16)))
        return dict(stg=("e_dve", vstg),
                    scr=[("e_dve", V["e_dve"]), ("s_sca", V["s_sca"]),
                         ("s_scb", V["s_scb"]), ("e_dve2", V["e_dve2"]),
                         ("s_sy", V["s_sy"])])


def _prep_inputs(inputs):
    sd = sigma_d_perm()
    x = inputs["L0_attn_x"]
    x2d = x.reshape(16384, 4096)
    xfm = np.ascontiguousarray(x2d.T[sd])

    W1 = inputs["L0_attn_in_proj_w"][:, sd]
    W1t = np.ascontiguousarray(
        W1.reshape(145, 128, 32, 128).transpose(0, 2, 3, 1)).reshape(145, -1)
    W1t = np.concatenate([W1t, np.zeros((1, W1t.shape[1]), np.float32)], 0)

    W2 = inputs["L0_attn_out_proj_w"][sd, :]
    W2t = np.ascontiguousarray(
        W2.reshape(32, 128, 64, 128).transpose(0, 2, 3, 1)).reshape(32, -1)

    jcols = [np.arange(j, INTER, 8) for j in range(8)]

    def pad_cols_i(a):
        o = np.zeros((a.shape[0], INTP), a.dtype)
        for j in range(8):
            o[:, j * 1408: j * 1408 + 1376] = a[:, jcols[j]]
        return o

    shared = dict(
        w1=W1t, w2=W2t,
        normw=np.ascontiguousarray(inputs["L0_attn_norm_w"][sd].reshape(32, 128).T),
        convw=np.ascontiguousarray(inputs["L0_attn_conv_w"].reshape(80, 128, 4)),
        convb=np.ascontiguousarray(inputs["L0_attn_conv_b"].reshape(80, 128).T),
        dtbias=np.ascontiguousarray(inputs["L0_attn_dt_bias"].reshape(128, 1)),
        alog=np.ascontiguousarray(inputs["L0_attn_A_log"].reshape(128, 1)),
        dprep=np.ascontiguousarray(np.repeat(inputs["L0_attn_Dp"], 64).reshape(64, 128).T),
        gnwi=np.ascontiguousarray(inputs["L0_attn_gnorm_w"].reshape(64, 128).T),
        gq=pad_cols_i(inputs["L0_ffn_gate_qweight"]),
        gsr=pad_cols_i(np.repeat(inputs["L0_ffn_gate_scales"], 16, axis=0)),
        gz=inputs["L0_ffn_gate_zeros"],
        gso=pad_cols_i(inputs["L0_ffn_gate_scales"]),
        uq=pad_cols_i(inputs["L0_ffn_up_qweight"]),
        usr=pad_cols_i(np.repeat(inputs["L0_ffn_up_scales"], 16, axis=0)),
        uz=inputs["L0_ffn_up_zeros"],
        uso=pad_cols_i(inputs["L0_ffn_up_scales"]),
        dq=np.concatenate([inputs["L0_ffn_down_qweight"][:, sd],
                           np.zeros((32, 4096), np.int32)], 0),
        dsr=np.concatenate([np.repeat(inputs["L0_ffn_down_scales"], 16, axis=0)[:, sd],
                            np.zeros((32, 4096), np.float16)], 0),
        dz=inputs["L0_ffn_down_zeros"],
        dso=np.ascontiguousarray(inputs["L0_ffn_down_scales"][:, sd]),
    )

    in_maps = []
    for k in range(8):
        xc = np.zeros((4096, W_ALL), np.float32)
        t0 = 2048 * k
        xc[:, HW:] = xfm[:, t0:t0 + T]
        fl = np.zeros((128, 1), np.float32)
        if k not in (0, 4):
            xc[:, 1:HW] = xfm[:, t0 - TH - 3: t0]
            fl[:] = 1.0
        m = dict(shared)
        m["xin"] = np.ascontiguousarray(xc.reshape(32, 128, W_ALL))
        m["flag"] = fl
        in_maps.append(m)
    return in_maps


def kernel(**inputs):
    in_maps = _prep_inputs(inputs)
    nc = build_program()
    res = run_bass_kernel_spmd(nc, in_maps, core_ids=list(range(8)))
    sd = sigma_d_perm()
    outs = [np.asarray(res.results[k]["out"]).reshape(4096, T) for k in range(8)]
    ofm = np.concatenate(outs, axis=1)
    out2d = np.empty((16384, 4096), np.float32)
    out2d[:, sd] = ofm.T
    return out2d.reshape(2, 8192, 4096)


if __name__ == "__main__":
    import time
    t0_ = time.time()
    nc = build_program(debug=False)
    n = sum(1 for _ in nc.all_instructions())
    print(f"built {n} instructions in {time.time()-t0_:.1f}s")
